# version 06
# baseline (speedup 1.0000x reference)
import contextlib
import numpy as np
import ml_dtypes
import concourse.bass as bass
import concourse.mybir as mybir
from concourse.bass_utils import run_bass_kernel_spmd

F32 = mybir.dt.float32
BF16 = mybir.dt.bfloat16
AF = mybir.ActivationFunctionType
ALU = mybir.AluOpType

NCORES = 8
NB = 2
S = 2048
D = 1024
NCH = 8
H = 16
DH = 64
WIN = 256
DI = 2730
NKS = 22
TL = 512
NTL = 4
NEG = -1.0e9

_CFG = {"with_main": True, "with_slot": True, "n_local": 6}

ENGS = ("pe", "act", "dve", "pool", "sp")


class _Op:
    __slots__ = ("fn", "eng", "waits", "inc_flag", "dma_stream", "ms", "seq")

    def __init__(self, eng, fn):
        self.eng = eng
        self.fn = fn
        self.waits = []
        self.inc_flag = False
        self.dma_stream = None
        self.ms = None
        self.seq = 0


class _Tok:
    __slots__ = ("kind", "eng", "op", "stream", "val")

    def __init__(self, kind, eng, op=None, stream=None, val=None):
        self.kind = kind
        self.eng = eng
        self.op = op
        self.stream = stream
        self.val = val


class _Buf:
    __slots__ = ("lo", "hi", "writer", "readers")

    def __init__(self, lo, hi, writer=None, readers=None):
        self.lo = lo
        self.hi = hi
        self.writer = writer
        self.readers = readers if readers is not None else []


class Prog:
    def __init__(self):
        self.ops = {e: [] for e in ENGS}
        self.dma_counts = {}
        self.live = {}
        self.psum_pending = {}
        self.seq = 0

    def _need(self, op, tok):
        if tok.kind == "op":
            if tok.op.eng == op.eng:
                return
            tok.op.inc_flag = True
            op.waits.append(tok)
            return
        op.waits.append(_Tok("dma", tok.eng, stream=tok.stream,
                             val=self.dma_counts[tok.stream]))

    def _acc(self, tok, op, reads, writes):
        for (key, lo, hi) in reads:
            for b in self.live.get(key, ()):
                if b.lo < hi and lo < b.hi:
                    if b.writer is not None:
                        self._need(op, b.writer)
                    b.readers.append(tok)
        for (key, lo, hi) in writes:
            lst = self.live.setdefault(key, [])
            keep, deps = [], []
            for b in lst:
                if b.lo < hi and lo < b.hi:
                    if b.writer is not None:
                        deps.append(b.writer)
                    deps.extend(b.readers)
                    if b.lo < lo:
                        keep.append(_Buf(b.lo, lo, b.writer, list(b.readers)))
                    if b.hi > hi:
                        keep.append(_Buf(hi, b.hi, b.writer, list(b.readers)))
                else:
                    keep.append(b)
            for t in deps:
                if t is tok:
                    continue
                self._need(op, t)
            keep.append(_Buf(lo, hi, tok, []))
            self.live[key] = keep

    def op(self, eng, fn, r=(), w=(), waits=()):
        o = _Op(eng, fn)
        self.seq += 1
        o.seq = self.seq
        for t in waits:
            self._need(o, t)
        tok = _Tok("op", eng, op=o)
        self._acc(tok, o, r, w)
        self.ops[eng].append(o)
        return tok

    def dma(self, eng, stream, fn, r=(), w=(), waits=()):
        o = _Op(eng, fn)
        self.seq += 1
        o.seq = self.seq
        o.dma_stream = stream
        self.dma_counts.setdefault(stream, 0)
        for t in waits:
            self._need(o, t)
        tok = _Tok("dma", eng, stream=stream, val=None)
        self._acc(tok, o, r, w)
        self.dma_counts[stream] += 16
        self.ops[eng].append(o)
        return tok

    def fence(self, keys):
        latest = {}
        bufs = []
        for (key, lo, hi) in keys:
            for b in self.live.get(key, ()):
                if b.lo < hi and lo < b.hi and b.writer is not None:
                    t = b.writer
                    g = ("op", t.eng) if t.kind == "op" else ("dma", t.stream)
                    cur = latest.get(g)
                    if t.kind == "op":
                        if cur is None or t.op.seq > cur.op.seq:
                            latest[g] = t
                    else:
                        if cur is None or (t.val or self.dma_counts[t.stream]) > \
                                (cur.val or self.dma_counts[cur.stream]):
                            latest[g] = t
                    bufs.append((b, g))
        for b, g in bufs:
            b.writer = latest[g]

    def group_start(self, bank):
        assert not self.psum_pending.get(bank), f"bank {bank} pending"
        self.psum_pending[bank] = True

    def group_stop(self, bank):
        assert self.psum_pending.get(bank), f"bank {bank} not started"
        self.psum_pending[bank] = False

    def finalize(self, nc, stack):
        self.psem = {}
        self.dsem = {}
        for e in ENGS:
            if any(o.inc_flag for o in self.ops[e]):
                self.psem[e] = stack.enter_context(nc.semaphore(f"pg_{e}"))
        for s in self.dma_counts:
            self.dsem[s] = stack.enter_context(nc.semaphore(f"dm_{s}"))
        for e in ENGS:
            k = 0
            for o in self.ops[e]:
                if o.inc_flag:
                    k += 1
                    o.ms = k

    def replay(self, engname, eng):
        waited = {}
        for o in self.ops[engname]:
            for t in o.waits:
                if t.kind == "op":
                    sem, val = self.psem[t.eng], t.op.ms
                else:
                    sem, val = self.dsem[t.stream], t.val
                if waited.get(id(sem), -1) < val:
                    eng.wait_ge(sem, val)
                    waited[id(sem)] = val
            inst = o.fn(eng)
            if o.dma_stream is not None:
                inst.then_inc(self.dsem[o.dma_stream], 16)
            elif o.inc_flag:
                assert inst is not None
                inst.then_inc(self.psem[engname], 1)


def _bf(x):
    return np.asarray(x).astype(ml_dtypes.bfloat16)


def _rotary_tables():
    half = np.arange(0, DH, 2, dtype=np.float64)
    inv_freq = 1.0 / 10000.0 ** (half / DH)
    t = np.arange(S, dtype=np.float64)
    freqs = t[:, None] * inv_freq[None, :]
    freqs = np.concatenate([freqs, freqs], -1)
    base_scale = (half + 0.4 * DH) / (1.4 * DH)
    power = (t - S // 2) / (WIN // 2)
    scale = base_scale[None, :] ** power[:, None]
    scale = np.concatenate([scale, scale], -1)
    cos, sin = np.cos(freqs), np.sin(freqs)
    inv = 1.0 / scale
    sign = np.where(np.arange(64) < 32, -1.0, 1.0)[None, :]
    tabs = [cos * scale, sin * scale * sign, cos * inv, sin * inv * sign]
    out = np.zeros((4, 128, S), np.float32)
    for i, tb in enumerate(tabs):
        fm = tb.T.astype(np.float32)
        out[i, :64] = fm
        out[i, 64:] = fm
    return _bf(out)


def _local_masks():
    m = np.zeros((4, 128, 256), np.float32)
    p = np.arange(128)[:, None]
    n = np.arange(256)[None, :]
    keeps = [
        (n < 128) & (p >= n),
        (n < 128) | (p >= n - 128),
        (n >= 128) | (p <= n),
        (n >= 128) & (p <= n - 128),
    ]
    for i, k in enumerate(keeps):
        m[i] = np.where(k, 0.0, NEG)
    return _bf(m)


def _main_masks():
    m = np.zeros((4, 128, 512), np.float32)
    p = np.arange(128)[:, None]
    n = np.arange(512)[None, :]
    for r in range(4):
        m[r] = np.where(p <= n - 128 * r, 0.0, NEG)
    return _bf(m)


def _cols(vec, width):
    L, n = np.asarray(vec).shape
    v = np.pad(np.asarray(vec, np.float32), ((0, 0), (0, width * 128 - n)))
    return np.ascontiguousarray(v.reshape(L, width, 128).transpose(0, 2, 1))


def build_program(cfg):
    nc = bass.Bass("TRN2", target_bir_lowering=False)
    stack = contextlib.ExitStack()

    dram = {}

    def din(name, shape, dt=F32, kind="ExternalInput"):
        dram[name] = nc.dram_tensor(name, shape, dt, kind=kind).ap()
        return dram[name]

    d_x = din("x", [NB, NCH, 128, S])
    d_out = din("out", [NB, NCH, 128, S], kind="ExternalOutput")
    d_wqt = din("wqt", [8, D, 128])
    d_wkt = din("wkt", [8, D, 128])
    d_wvt = din("wvt", [8, D, 128])
    d_wot = din("wot", [D, D])
    d_wqkv = din("wqkv", [6, D, 3 * D])
    d_wol = din("wol", [6, D, D])
    d_w1 = din("w1", [6, D, 2 * DI])
    d_w2 = din("w2", [6, DI, D])
    d_sm = din("sm", [6, 128, 72])
    d_tab = din("tab", [4, 128, S], BF16)
    d_mloc = din("mloc", [4, 128, 256], BF16)
    d_mmain = din("mmain", [4, 128, 512], BF16)
    d_ident = din("ident", [128, 128], BF16)
    d_bd2 = din("bd2", [128, 2], BF16)
    d_bd2t = din("bd2t", [2, 128], BF16)
    d_ones = din("onesr", [1, 512], BF16)
    d_x1s = din("x1scr", [NCH, 128, S], F32, kind="Internal")
    n_taps = int(cfg.get("taps", 0))
    d_dbg = din("dbg", [max(n_taps, 1), NCH, 128, S], F32, kind="ExternalOutput") \
        if n_taps else None

    A_F = 17500
    A_B = 65152
    A_C = 4870
    fa = stack.enter_context(nc.sbuf_tensor("fa", [128, A_F], F32))
    ba = stack.enter_context(nc.sbuf_tensor("ba", [128, A_B], BF16))
    cb = stack.enter_context(nc.sbuf_tensor("cb", [128, A_C], BF16))
    psum = [stack.enter_context(nc.psum_tensor(f"pb{i}", [128, 512], F32))
            for i in range(7)]
    psbf = stack.enter_context(nc.psum_tensor("pb7", [128, 1024], BF16))

    oXR = 0
    oR0 = 16384
    oR1 = 16896
    oSM = 17408
    oZC = 17408 + 72
    oEPS = oZC + 1
    oBND = oEPS + 1
    assert oBND + 12 <= A_F

    cID, cBD2, cBD2T, cONES, cOC = 0, 128, 130, 258, 770
    cMSK = 771
    cROWS = 2819
    assert cROWS + 2048 <= A_C

    L_WQKV = 0
    L_WO = 24576
    L_Q = 32768
    L_KR = 36864
    L_VN = 45056
    L_XS = 53376
    L_TAB = 57472
    L_OW = 59520
    L_PT = 61568
    L_SC = 62592
    L_VS = 63616
    L_IB = 64128
    L_IB2 = 64640
    assert L_IB2 + 512 <= A_B
    M_XN = 0
    M_OALL = 16384
    M_QP = 32768
    M_KP = 34816
    M_VN = 36864
    M_W = 38944
    M_WOT = 45088
    M_PT = 53280
    M_VS = 55328
    M_SQ = 55840
    M_MU = 56352
    M_ISD = 56864
    assert M_ISD + 512 <= A_B
    F_XN = 0
    F_H = 8192
    F_W2 = 30720
    F_W1 = 53248
    F_GT = 57344
    F_SQ = 58368
    F_MU = 58880
    F_ISD = 59392
    E_XC = 0
    E_SQ = 4096
    E_MU = 4608
    E_ISD = 5120

    P = Prog()

    def BA(lo, n): return ("ba", lo, lo + n)
    def FA(lo, n): return ("fa", lo, lo + n)
    def CB(lo, n): return ("cb", lo, lo + n)
    def PS(i): return ("ps", i, i + 1)

    def FX(c):
        return fa[:, oXR + c * S: oXR + (c + 1) * S]

    ident = cb[:, cID:cID + 128]
    bd2 = cb[:, cBD2:cBD2 + 2]
    bd2t = cb[0:2, cBD2T:cBD2T + 128]
    ones_row = cb[0:1, cONES:cONES + 512]
    onescol = cb[:, cOC:cOC + 1]

    P.last_ldw = None

    def mm(bank, out_ap, lhsT, rhs, start, stop, r, tp=None, ldw_key=None):
        if start:
            P.group_start(bank)
        if stop:
            P.group_stop(bank)
        key = (ldw_key, tp) if ldw_key is not None else None
        if key is None or key != P.last_ldw:
            P.op("pe", lambda e, l=lhsT, tp=tp: e.ldweights(l, tile_position=tp),
                 r=list(r))
        P.last_ldw = key
        reads = list(r)
        writes = []
        if start or stop:
            writes = [PS(bank)]
        if not start:
            reads = reads + [PS(bank)]
        return P.op("pe",
                    lambda e, o=out_ap, l=lhsT, rh=rhs, st=start, sp=stop, tp=tp:
                    e.matmul(o, l, rh, start=st, stop=sp, tile_position=tp),
                    r=reads, w=writes)

    P.dma("sp", "io", lambda e: e.dma_start(out=ident, in_=d_ident), w=[CB(cID, 128)])
    P.dma("sp", "io", lambda e: e.dma_start(out=bd2, in_=d_bd2), w=[CB(cBD2, 2)])
    P.dma("sp", "io", lambda e: e.dma_start(out=bd2t, in_=d_bd2t), w=[CB(cBD2T, 128)])
    P.dma("sp", "io", lambda e: e.dma_start(out=ones_row, in_=d_ones), w=[CB(cONES, 512)])
    P.op("dve", lambda e: e.memset(onescol, 1.0), w=[CB(cOC, 1)])
    P.op("dve", lambda e: e.memset(fa[:, oZC:oZC + 1], 0.0), w=[FA(oZC, 1)])
    P.op("dve", lambda e: e.memset(fa[:, oEPS:oEPS + 1], 1e-5), w=[FA(oEPS, 1)])

    def load_masks(which):
        src = d_mmain if which == "main" else d_mloc
        wd = 512 if which == "main" else 256
        for i in range(4):
            P.dma("sp", "io",
                  lambda e, i=i, src=src, wd=wd: e.dma_start(
                      out=cb[:, cMSK + i * wd: cMSK + (i + 1) * wd], in_=src[i]),
                  w=[CB(cMSK + i * wd, wd)] if i else [CB(cMSK, 2048)])

    def mask_ap(which, i):
        wd = 512 if which == "main" else 256
        return cb[:, cMSK + i * wd: cMSK + i * wd + wd], CB(cMSK + i * wd, wd)

    def preset_local_ones():
        for h in range(H):
            for s_ in range(8):
                off = L_VN + h * 520 + s_ * 65 + 64
                P.op("dve", lambda e, off=off: e.memset(ba[:, off:off + 1], 1.0),
                     w=[BA(off, 1)])

    def preset_main_ones():
        for h in range(2):
            for s_ in range(16):
                off = M_VN + h * 1040 + s_ * 65 + 64
                P.op("dve", lambda e, off=off: e.memset(ba[:, off:off + 1], 1.0),
                     w=[BA(off, 1)])

    def load_smalls(l):
        P.dma("sp", "io", lambda e, l=l: e.dma_start(
            out=fa[:, oSM:oSM + 72], in_=d_sm[l]), w=[FA(oSM, 72)])

    def smcol(i, parts=128):
        return fa[0:parts, oSM + i:oSM + i + 1]

    SM_R = FA(oSM, 72)

    def shift_view(c, lo, n, shifted):
        if (not shifted) or c < 4:
            return FX(c)[:, lo:lo + n], FA(oXR + c * S + lo, n), None, None
        main = FX(c)[:, lo:lo + n - 1]
        mkey = FA(oXR + c * S + lo, n - 1)
        if lo == 0:
            return main, mkey, fa[:, oZC:oZC + 1], FA(oZC, 1)
        cell = oBND + (c - 4) * 3 + (lo // TL - 1)
        return main, mkey, fa[:, cell:cell + 1], FA(cell, 1)

    def emit_stash_bnd():
        for c in range(4, NCH):
            for bi, col in enumerate((511, 1023, 1535)):
                cell = oBND + (c - 4) * 3 + bi
                P.op("dve", lambda e, c=c, col=col, cell=cell: e.tensor_copy(
                    fa[:, cell:cell + 1], FX(c)[:, col:col + 1]),
                    r=[FA(oXR + c * S + col, 1)], w=[FA(cell, 1)])

    def emit_ln(tiles, dst, shifted, layer=None, inplace=False,
                sq_off=None, mu_off=None, isd_off=None):
        for ti, t in enumerate(tiles):
            lo = t * TL
            for c in range(NCH):
                dlo = dst(ti, c)
                d = ba[:, dlo:dlo + TL]
                sv, rk, bnd, bk = shift_view(c, lo, TL, shifted)
                if bnd is not None:
                    P.op("dve", lambda e, d=d, s=sv: e.tensor_copy(d[:, 1:TL], s),
                         r=[rk], w=[BA(dlo + 1, TL - 1)])
                    P.op("dve", lambda e, d=d, b_=bnd: e.tensor_copy(d[:, 0:1], b_),
                         r=[bk], w=[BA(dlo, 1)])
                else:
                    P.op("dve", lambda e, d=d, s=sv: e.tensor_copy(d, s),
                         r=[rk], w=[BA(dlo, TL)])
            for c in range(NCH):
                dlo = dst(ti, c)
                d = ba[:, dlo:dlo + TL]
                sq = ba[:, sq_off:sq_off + TL]
                P.op("dve", lambda e, d=d, q=sq: e.tensor_mul(q, d, d),
                     r=[BA(dlo, TL)], w=[BA(sq_off, TL)])
                mm(5, psum[5][0:1, 0:TL], onescol, d, c == 0, c == NCH - 1,
                   r=[BA(dlo, TL), CB(cOC, 1)])
                mm(6, psum[6][0:1, 0:TL], onescol, sq, c == 0, c == NCH - 1,
                   r=[BA(sq_off, TL), CB(cOC, 1)])
            mu = fa[0:1, oR0:oR0 + TL]
            e2 = fa[0:1, oR1:oR1 + TL]
            P.op("dve", lambda e, m=mu: e.tensor_scalar_mul(m, psum[5][0:1, 0:TL], 1.0 / D),
                 r=[PS(5)], w=[FA(oR0, TL)])
            P.op("dve", lambda e, v=e2: e.tensor_scalar_mul(v, psum[6][0:1, 0:TL], 1.0 / D),
                 r=[PS(6)], w=[FA(oR1, TL)])
            mu_bf = cb[0:1, cROWS:cROWS + TL]
            P.op("act", lambda e, m=mu, mb=mu_bf: e.activation(mb, m, AF.Identity),
                 r=[FA(oR0, TL)], w=[CB(cROWS, TL)])
            P.op("dve", lambda e, m=mu: e.scalar_tensor_tensor(
                m, m, -1.0, m, ALU.mult, ALU.mult), r=[FA(oR0, TL)], w=[FA(oR0, TL)])
            P.op("dve", lambda e, v=e2, m=mu: e.tensor_add(v, v, m),
                 r=[FA(oR0, TL), FA(oR1, TL)], w=[FA(oR1, TL)])
            P.op("act", lambda e, v=e2: e.activation(
                v, v, AF.Ln, bias=fa[0:1, oEPS:oEPS + 1]),
                r=[FA(oR1, TL), FA(oEPS, 1)], w=[FA(oR1, TL)])
            isd_bf = cb[0:1, cROWS + TL:cROWS + 2 * TL]
            P.op("act", lambda e, v=e2, ib=isd_bf: e.activation(ib, v, AF.Exp, scale=-0.5),
                 r=[FA(oR1, TL)], w=[CB(cROWS + TL, TL)])
            mm(5, psum[5][:, 0:TL], ones_row[0:1, 0:128], mu_bf, True, True,
               r=[CB(cROWS, TL), CB(cONES, 128)])
            muB = ba[:, mu_off:mu_off + TL]
            P.op("dve", lambda e, m=muB: e.tensor_copy(m, psum[5][:, 0:TL]),
                 r=[PS(5)], w=[BA(mu_off, TL)])
            mm(6, psum[6][:, 0:TL], ones_row[0:1, 0:128], isd_bf, True, True,
               r=[CB(cROWS + TL, TL), CB(cONES, 128)])
            isdB = ba[:, isd_off:isd_off + TL]
            P.op("dve", lambda e, i_=isdB: e.tensor_copy(i_, psum[6][:, 0:TL]),
                 r=[PS(6)], w=[BA(isd_off, TL)])
            for c in range(NCH):
                dlo = dst(ti, c)
                d = ba[:, dlo:dlo + TL]
                if inplace:
                    tgt = FX(c)[:, lo:lo + TL]
                    P.op("dve", lambda e, t_=tgt, m=muB: e.tensor_sub(t_, t_, m),
                         r=[BA(mu_off, TL), FA(oXR + c * S + lo, TL)],
                         w=[FA(oXR + c * S + lo, TL)])
                    P.op("dve", lambda e, t_=tgt, i_=isdB: e.tensor_mul(t_, t_, i_),
                         r=[BA(isd_off, TL), FA(oXR + c * S + lo, TL)],
                         w=[FA(oXR + c * S + lo, TL)])
                else:
                    sv, rk, bnd, bk = shift_view(c, lo, TL, shifted)
                    if bnd is not None:
                        P.op("dve", lambda e, d=d, s=sv, m=muB: e.tensor_sub(
                            d[:, 1:TL], s, m[:, 1:TL]),
                            r=[rk, BA(mu_off, TL)], w=[BA(dlo, TL)])
                        P.op("dve", lambda e, d=d, b_=bnd, m=muB: e.tensor_sub(
                            d[:, 0:1], b_, m[:, 0:1]),
                            r=[bk, BA(mu_off, TL)], w=[BA(dlo, 1)])
                    else:
                        P.op("dve", lambda e, d=d, s=sv, m=muB: e.tensor_sub(d, s, m),
                             r=[rk, BA(mu_off, TL)], w=[BA(dlo, TL)])
                    P.op("dve", lambda e, d=d, i_=isdB: e.tensor_mul(d, d, i_),
                         r=[BA(isd_off, TL)], w=[BA(dlo, TL)])
                    if layer is not None:
                        P.op("act", lambda e, d=d, c=c, l=layer: e.activation(
                            d, d, AF.Identity, scale=smcol(2 + c), bias=smcol(10 + c)),
                            r=[SM_R], w=[BA(dlo, TL)])
        if inplace:
            P.fence([FA(oXR, NCH * S)])
        else:
            w0 = dst(0, 0)
            wN = dst(len(list(tiles)) - 1, NCH - 1) + TL
            P.fence([BA(w0, wN - w0)])

    def emit_load_x(b):
        for c in range(NCH):
            P.dma("sp", "io", lambda e, b=b, c=c: e.dma_start(
                out=FX(c), in_=d_x[b, c]), w=[FA(oXR + c * S, S)])

    def emit_store(b):
        toks = []
        for c in range(NCH):
            toks.append(P.dma("sp", "io", lambda e, b=b, c=c: e.dma_start(
                out=d_out[b, c], in_=FX(c)), r=[FA(oXR + c * S, S)]))
        return toks

    def emit_main_attn():
        load_masks("main")
        preset_main_ones()
        P.dma("pool", "w", lambda e: e.dma_start(
            out=ba[:, M_WOT:M_WOT + 8192].rearrange("p (c m) -> p c m", c=8),
            in_=d_wot.rearrange("(c p) m -> p c m", p=128)),
            w=[BA(M_WOT, 8192)])
        for hp in range(8):
            wb = M_W + (hp % 2) * 3072
            for wi, dsrc in enumerate((d_wqt, d_wkt, d_wvt)):
                P.dma("pool", "w", lambda e, hp=hp, wb=wb, wi=wi, dsrc=dsrc:
                      e.dma_start(
                          out=ba[:, wb + wi * 1024: wb + (wi + 1) * 1024]
                          .rearrange("p (c m) -> p c m", c=8),
                          in_=dsrc[hp].rearrange("(c p) m -> p c m", p=128)),
                      w=[BA(wb + wi * 1024, 1024)])
            for wi, dst_off in enumerate((M_QP, M_KP, None)):
                for k in range(NCH):
                    lhsT = ba[:, wb + wi * 1024 + k * 128: wb + wi * 1024 + (k + 1) * 128]
                    for qt in range(NTL):
                        rhs = ba[:, M_XN + k * S + qt * TL: M_XN + k * S + qt * TL + TL]
                        mm(qt, psum[qt][:, 0:TL], lhsT, rhs, k == 0, k == NCH - 1,
                           r=[BA(wb + wi * 1024 + k * 128, 128),
                              BA(M_XN + k * S + qt * TL, TL)],
                           ldw_key=("mproj", hp, wi, k))
                for qt in range(NTL):
                    bank = qt
                    if dst_off is not None:
                        dl = dst_off + qt * TL
                        P.op("act", lambda e, dl=dl, bank=bank: e.activation(
                            ba[:, dl:dl + TL], psum[bank][:, 0:TL], AF.Identity),
                            r=[PS(bank)], w=[BA(dl, TL)])
                    else:
                        P.op("act", lambda e, bank=bank: e.activation(
                            ba[:, M_VS:M_VS + TL], psum[bank][:, 0:TL], AF.Identity),
                            r=[PS(bank)], w=[BA(M_VS, TL)])
                        P.last_ldw = None
                        for cb4 in range(4):
                            kb = qt * 4 + cb4
                            pslot = (cb4 % 2) * 128
                            P.op("pe", lambda e, cb4=cb4, pslot=pslot:
                                 e.transpose(psbf[:, pslot:pslot + 128],
                                             ba[:, M_VS + cb4 * 128:M_VS + (cb4 + 1) * 128],
                                             ident),
                                 r=[BA(M_VS, TL), CB(cID, 128)], w=[PS(7)])
                            for h2 in range(2):
                                off = M_VN + h2 * 1040 + kb * 65
                                P.op("dve", lambda e, off=off, pslot=pslot, h2=h2:
                                     e.tensor_copy(ba[:, off:off + 64],
                                                   psbf[:, pslot + h2 * 64:pslot + h2 * 64 + 64]),
                                     r=[PS(7)], w=[BA(off, 64)])
            P.fence([BA(M_QP, 2048), BA(M_KP, 2048), BA(M_VN, 2080)])
            for qt in range(NTL):
                nkb = qt * 4 + 4

                def st_mm(kb):
                    bA = kb % 2
                    bB = 2 + kb % 2
                    masked = kb >= qt * 4
                    mm(bA, psum[bA][:, 0:TL],
                       ba[0:64, M_KP + kb * 128: M_KP + (kb + 1) * 128],
                       ba[0:64, M_QP + qt * TL: M_QP + qt * TL + TL],
                       True, not masked,
                       r=[BA(M_KP + kb * 128, 128), BA(M_QP + qt * TL, TL)])
                    mm(bB, psum[bB][:, 0:TL],
                       ba[64:128, M_KP + kb * 128: M_KP + (kb + 1) * 128],
                       ba[64:128, M_QP + qt * TL: M_QP + qt * TL + TL],
                       True, not masked, tp=(64, 0),
                       r=[BA(M_KP + kb * 128, 128), BA(M_QP + qt * TL, TL)])
                    if masked:
                        mk, mkr = mask_ap("main", kb - qt * 4)
                        mm(bA, psum[bA][:, 0:TL], ident, mk, False, True,
                           r=[CB(cID, 128), mkr], ldw_key="ident")
                        mm(bB, psum[bB][:, 0:TL], ident, mk, False, True,
                           r=[CB(cID, 128), mkr], ldw_key="ident")
                    pA = M_PT + (kb % 2) * TL
                    pB = M_PT + 1024 + (kb % 2) * TL
                    P.op("act", lambda e, pA=pA, bA=bA: e.activation(
                        ba[:, pA:pA + TL], psum[bA][:, 0:TL], AF.Exp, scale=0.125),
                        r=[PS(bA)], w=[BA(pA, TL)])
                    P.op("act", lambda e, pB=pB, bB=bB: e.activation(
                        ba[:, pB:pB + TL], psum[bB][:, 0:TL], AF.Exp, scale=0.125),
                        r=[PS(bB)], w=[BA(pB, TL)])

                def pv_mm(kb):
                    pA = M_PT + (kb % 2) * TL
                    pB = M_PT + 1024 + (kb % 2) * TL
                    for h2, (pof, ob) in enumerate(((pA, 4), (pB, 5))):
                        vno = M_VN + h2 * 1040 + kb * 65
                        mm(ob, psum[ob][0:65, 0:TL], ba[:, vno:vno + 65],
                           ba[:, pof:pof + TL], kb == 0, kb == nkb - 1,
                           r=[BA(vno, 65), BA(pof, TL)])

                st_mm(0)
                for kb in range(nkb):
                    if kb + 1 < nkb:
                        st_mm(kb + 1)
                    pv_mm(kb)
                par = qt % 2
                oo = M_OALL + hp * S + qt * TL
                for h2, ob in enumerate((4, 5)):
                    P.op("dve", lambda e, oo=oo, h2=h2, ob=ob: e.tensor_copy(
                        ba[h2 * 64:(h2 + 1) * 64, oo:oo + TL],
                        psum[ob][0:64, 0:TL]),
                        r=[PS(ob)], w=[BA(oo, TL)])
                    ro = (oR0, oR1)[par]
                    rr = fa[0:1, ro:ro + TL]
                    P.op("act", lambda e, ob=ob, rr=rr: e.activation(
                        rr, psum[ob][64:65, 0:TL], AF.Ln), r=[PS(ob)], w=[FA(ro, TL)])
                    rbo = cROWS + (par * 2 + h2) * TL
                    rb = cb[0:1, rbo:rbo + TL]
                    P.op("act", lambda e, rr=rr, rb=rb: e.activation(
                        rb, rr, AF.Exp, scale=-1.0), r=[FA(ro, TL)], w=[CB(rbo, TL)])
                    mm(6, psum[6][0:64, 0:TL], ones_row[0:1, 0:64], rb, True, True,
                       r=[CB(rbo, TL), CB(cONES, 128)])
                    ibo = (M_SQ, M_MU)[h2]
                    ib = ba[h2 * 64:(h2 + 1) * 64, ibo:ibo + TL]
                    P.op("dve", lambda e, ib=ib, h2=h2: e.tensor_copy(
                        ib, psum[6][0:64, 0:TL]), r=[PS(6)], w=[BA(ibo, TL)])
                    P.op("dve", lambda e, oo=oo, h2=h2, ib=ib: e.tensor_mul(
                        ba[h2 * 64:(h2 + 1) * 64, oo:oo + TL],
                        ba[h2 * 64:(h2 + 1) * 64, oo:oo + TL], ib),
                        r=[BA(ibo, TL), BA(oo, TL)], w=[BA(oo, TL)])
        P.fence([BA(M_OALL, 16384)])
        for m in range(NCH):
            for k in range(NCH):
                lhsT = ba[:, M_WOT + k * 1024 + m * 128: M_WOT + k * 1024 + (m + 1) * 128]
                for qt in range(NTL):
                    rhs = ba[:, M_OALL + k * S + qt * TL: M_OALL + k * S + qt * TL + TL]
                    mm(qt, psum[qt][:, 0:TL], lhsT, rhs, k == 0, k == NCH - 1,
                       r=[BA(M_WOT + k * 1024 + m * 128, 128),
                          BA(M_OALL + k * S + qt * TL, TL)],
                       ldw_key=("mwo", m, k))
            for qt in range(NTL):
                tgt = FX(m)[:, qt * TL:(qt + 1) * TL]
                P.op("dve", lambda e, tgt=tgt, qt=qt: e.tensor_add(
                    tgt, psum[qt][:, 0:TL], tgt),
                    r=[PS(qt), FA(oXR + m * S + qt * TL, TL)],
                    w=[FA(oXR + m * S + qt * TL, TL)])
    def emit_local_attn(l, prefetched=False):
        preset_local_ones()
        emit_stash_bnd()
        prev = None
        for qd in range(4):
            mlo = qd * 6
            wkeys = [("ba", L_WQKV + c * 3072 + mlo * 128,
                      L_WQKV + c * 3072 + (mlo + 6) * 128) for c in range(NCH)]
            tok = P.dma("pool", f"wq{qd}",
                        lambda e, l=l, mlo=mlo: e.dma_start(
                            out=ba[:, L_WQKV + 0:L_WQKV + 24576]
                            .rearrange("p (c m) -> p c m", c=8)[:, :, mlo * 128:(mlo + 6) * 128],
                            in_=d_wqkv[l].rearrange("(c p) m -> p c m", p=128)
                            [:, :, mlo * 128:(mlo + 6) * 128]),
                        w=wkeys,
                        waits=[prev] if prev is not None else ())
            prev = tok
        P.dma("pool", "wo", lambda e, l=l: e.dma_start(
            out=ba[:, L_WO:L_WO + 8192].rearrange("p (c m) -> p c m", c=8),
            in_=d_wol[l].rearrange("(c p) m -> p c m", p=128)),
            w=[BA(L_WO, 8192)], waits=[prev])
        for t in range(NTL):
            lo = t * TL
            for i in range(4):
                P.dma("sp", "io", lambda e, i=i, lo=lo: e.dma_start(
                    out=ba[:, L_TAB + i * TL: L_TAB + (i + 1) * TL],
                    in_=d_tab[i][:, lo:lo + TL]),
                    w=[BA(L_TAB + i * TL, TL)])
            for c in range(NCH):
                dlo = L_XS + c * TL
                d = ba[:, dlo:dlo + TL]
                sv, rk, bnd, bk = shift_view(c, lo, TL, True)
                if bnd is not None:
                    P.op("dve", lambda e, d=d, s=sv: e.tensor_copy(d[:, 1:TL], s),
                         r=[rk], w=[BA(dlo + 1, TL - 1)])
                    P.op("dve", lambda e, d=d, b_=bnd: e.tensor_copy(d[:, 0:1], b_),
                         r=[bk], w=[BA(dlo, 1)])
                else:
                    P.op("dve", lambda e, d=d, s=sv: e.tensor_copy(d, s),
                         r=[rk], w=[BA(dlo, TL)])
            P.fence([BA(L_XS, 4096)])

            def norm_stage1(which, c, par):
                if which == 0:
                    slo = L_Q + c * TL
                else:
                    slo = L_KR + c * 1024 + (t % 2) * TL
                src = ba[:, slo:slo + TL]
                sqo = L_SC + par * TL
                sq = ba[:, sqo:sqo + TL]
                stat_bank = (4, 6)[par]
                P.op("dve", lambda e, s=src, q=sq: e.tensor_mul(q, s, s),
                     r=[BA(slo, TL)], w=[BA(sqo, TL)])
                mm(stat_bank, psum[stat_bank][0:2, 0:TL], bd2, sq, True, True,
                   r=[BA(sqo, TL), CB(cBD2, 2)])
                ro = (oR0, oR1)[par]
                r2 = fa[0:2, ro:ro + TL]
                P.op("dve", lambda e, r2=r2, sb=stat_bank: e.tensor_scalar_max(
                    r2, psum[sb][0:2, 0:TL], 1e-24), r=[PS(stat_bank)], w=[FA(ro, TL)])
                P.op("act", lambda e, r2=r2: e.activation(r2, r2, AF.Ln),
                     r=[FA(ro, TL)], w=[FA(ro, TL)])
                rbo = cROWS + par * TL
                br = cb[0:2, rbo:rbo + TL]
                P.op("act", lambda e, r2=r2, br=br: e.activation(
                    br, r2, AF.Exp, scale=-0.5), r=[FA(ro, TL)], w=[CB(rbo, TL)])

            def norm_stage2(which, c, par):
                if which == 0:
                    slo = L_Q + c * TL
                    ct, st_ = L_TAB + 0 * TL, L_TAB + 1 * TL
                    scol = 0
                else:
                    slo = L_KR + c * 1024 + (t % 2) * TL
                    ct, st_ = L_TAB + 2 * TL, L_TAB + 3 * TL
                    scol = 1
                src = ba[:, slo:slo + TL]
                rbo = cROWS + par * TL
                br = cb[0:2, rbo:rbo + TL]
                bc_bank = (5, 3)[par]
                mm(bc_bank, psum[bc_bank][:, 0:TL], bd2t, br, True, True,
                   r=[CB(rbo, TL), CB(cBD2T, 128)])
                ib = ba[:, L_IB:L_IB + TL] if par == 0 else ba[:, L_IB2:L_IB2 + TL]
                ibk = BA(L_IB, TL) if par == 0 else BA(L_IB2, TL)
                P.op("dve", lambda e, ib=ib, bb=bc_bank: e.tensor_copy(
                    ib, psum[bb][:, 0:TL]), r=[PS(bc_bank)], w=[ibk])
                P.op("dve", lambda e, s=src, ib=ib: e.tensor_mul(s, s, ib),
                     r=[ibk, BA(slo, TL)], w=[BA(slo, TL)])
                P.op("act", lambda e, s=src, scol=scol: e.activation(
                    s, s, AF.Identity, scale=smcol(scol)),
                    r=[SM_R, BA(slo, TL)], w=[BA(slo, TL)])
                tro = L_SC + par * TL
                tr = ba[:, tro:tro + TL]
                for (d0, s0) in ((0, 32), (32, 0), (64, 96), (96, 64)):
                    P.op("dve", lambda e, d0=d0, s0=s0, s=src, tr=tr:
                         e.tensor_copy(tr[d0:d0 + 32, :], s[s0:s0 + 32, :]),
                         r=[BA(slo, TL)], w=[BA(tro, TL)])
                P.op("dve", lambda e, tr=tr, st_=st_: e.tensor_mul(
                    tr, tr, ba[:, st_:st_ + TL]),
                    r=[BA(st_, TL), BA(tro, TL)], w=[BA(tro, TL)])
                P.op("dve", lambda e, s=src, ct=ct: e.tensor_mul(
                    s, s, ba[:, ct:ct + TL]),
                    r=[BA(ct, TL), BA(slo, TL)], w=[BA(slo, TL)])
                P.op("dve", lambda e, s=src, tr=tr: e.tensor_add(s, s, tr),
                     r=[BA(tro, TL), BA(slo, TL)], w=[BA(slo, TL)])

            nchain = 0
            pending = []
            for mI in range(24):
                bank = mI % 3
                for k in range(NCH):
                    lhsT = ba[:, L_WQKV + k * 3072 + mI * 128: L_WQKV + k * 3072 + (mI + 1) * 128]
                    rhs = ba[:, L_XS + k * TL: L_XS + (k + 1) * TL]
                    mm(bank, psum[bank][:, 0:TL], lhsT, rhs, k == 0, k == NCH - 1,
                       r=[BA(L_WQKV + k * 3072 + mI * 128, 128), BA(L_XS + k * TL, TL)])
                if pending:
                    pending.pop(0)()
                if mI < 8:
                    dlo = L_Q + mI * TL
                    P.op("act", lambda e, dlo=dlo, bank=bank: e.activation(
                        ba[:, dlo:dlo + TL], psum[bank][:, 0:TL], AF.Identity),
                        r=[PS(bank)], w=[BA(dlo, TL)])
                    norm_stage1(0, mI, nchain % 2)
                    pending.append(lambda w_=0, c_=mI, p_=nchain % 2: norm_stage2(w_, c_, p_))
                    nchain += 1
                elif mI < 16:
                    c = mI - 8
                    dlo = L_KR + c * 1024 + (t % 2) * TL
                    P.op("act", lambda e, dlo=dlo, bank=bank: e.activation(
                        ba[:, dlo:dlo + TL], psum[bank][:, 0:TL], AF.Identity),
                        r=[PS(bank)], w=[BA(dlo, TL)])
                    norm_stage1(1, c, nchain % 2)
                    pending.append(lambda w_=1, c_=c, p_=nchain % 2: norm_stage2(w_, c_, p_))
                    nchain += 1
                else:
                    c = mI - 16
                    P.op("act", lambda e, bank=bank: e.activation(
                        ba[:, L_VS:L_VS + TL], psum[bank][:, 0:TL], AF.Identity),
                        r=[PS(bank)], w=[BA(L_VS, TL)])
                    P.last_ldw = None
                    for cb4 in range(4):
                        kb = (t * 4 + cb4) % 8
                        pslot = (cb4 % 2) * 128
                        P.op("pe", lambda e, cb4=cb4, pslot=pslot:
                             e.transpose(psbf[:, pslot:pslot + 128],
                                         ba[:, L_VS + cb4 * 128:L_VS + (cb4 + 1) * 128],
                                         ident),
                             r=[BA(L_VS, TL), CB(cID, 128)], w=[PS(7)])
                        for h2 in range(2):
                            off = L_VN + (2 * c + h2) * 520 + kb * 65
                            P.op("dve", lambda e, off=off, pslot=pslot, h2=h2:
                                 e.tensor_copy(ba[:, off:off + 64],
                                               psbf[:, pslot + h2 * 64:pslot + h2 * 64 + 64]),
                                 r=[PS(7)], w=[BA(off, 64)])
            for fn_ in pending:
                fn_()
            P.fence([BA(L_Q, 4096), BA(L_KR, 8192), BA(L_VN, 8320)])
            for wi in range(2):
                wloc = t * 2 + wi
                qcol = wi * 256
                kbs = [x for x in (2 * wloc - 2, 2 * wloc - 1, 2 * wloc, 2 * wloc + 1)
                       if x >= 0]
                mks = list(range(4 - len(kbs), 4))
                pend_w = []
                for hp in range(8):
                    def st_mm(j):
                        kb = kbs[j]
                        slot = kb % 8
                        bA, bB = j % 2, 2 + j % 2
                        klo = L_KR + hp * 1024 + slot * 128
                        qap_ = L_Q + hp * TL + qcol
                        mm(bA, psum[bA][:, 0:256], ba[0:64, klo:klo + 128],
                           ba[0:64, qap_:qap_ + 256], True, False,
                           r=[BA(klo, 128), BA(L_Q + hp * TL, TL)])
                        mm(bB, psum[bB][:, 0:256], ba[64:128, klo:klo + 128],
                           ba[64:128, qap_:qap_ + 256], True, False, tp=(64, 0),
                           r=[BA(klo, 128), BA(L_Q + hp * TL, TL)])
                        mk, mkr = mask_ap("local", mks[j])
                        mm(bA, psum[bA][:, 0:256], ident, mk, False, True,
                           r=[CB(cID, 128), mkr], ldw_key="ident")
                        mm(bB, psum[bB][:, 0:256], ident, mk, False, True,
                           r=[CB(cID, 128), mkr], ldw_key="ident")
                        pA = L_PT + (j % 2) * 256
                        pB = L_PT + 512 + (j % 2) * 256
                        P.op("act", lambda e, pA=pA, bA=bA: e.activation(
                            ba[:, pA:pA + 256], psum[bA][:, 0:256], AF.Exp, scale=8.0),
                            r=[PS(bA)], w=[BA(pA, 256)])
                        P.op("act", lambda e, pB=pB, bB=bB: e.activation(
                            ba[:, pB:pB + 256], psum[bB][:, 0:256], AF.Exp, scale=8.0),
                            r=[PS(bB)], w=[BA(pB, 256)])

                    def pv_mm(j):
                        kb = kbs[j]
                        slot = kb % 8
                        pA = L_PT + (j % 2) * 256
                        pB = L_PT + 512 + (j % 2) * 256
                        for h2, (pof, ob) in enumerate(((pA, 4), (pB, 5))):
                            vno = L_VN + (2 * hp + h2) * 520 + slot * 65
                            mm(ob, psum[ob][0:65, 0:256], ba[:, vno:vno + 65],
                               ba[:, pof:pof + 256], j == 0, j == len(kbs) - 1,
                               r=[BA(vno, 65), BA(pof, 256)])

                    if pend_w:
                        pend_w.pop(0)()

                    st_mm(0)
                    for j in range(len(kbs)):
                        if j + 1 < len(kbs):
                            st_mm(j + 1)
                        pv_mm(j)
                    par = hp % 2
                    oo = L_OW + hp * 256
                    for h2, ob in enumerate((4, 5)):
                        P.op("dve", lambda e, oo=oo, h2=h2, ob=ob: e.tensor_copy(
                            ba[h2 * 64:(h2 + 1) * 64, oo:oo + 256],
                            psum[ob][0:64, 0:256]),
                            r=[PS(ob)], w=[BA(oo, 256)])
                        ro = (oR0, oR1)[par]
                        rr = fa[0:1, ro:ro + 256]
                        P.op("act", lambda e, ob=ob, rr=rr: e.activation(
                            rr, psum[ob][64:65, 0:256], AF.Ln), r=[PS(ob)], w=[FA(ro, 256)])
                        rbo = cROWS + (par * 2 + h2) * TL
                        rb = cb[0:1, rbo:rbo + 256]
                        P.op("act", lambda e, rr=rr, rb=rb: e.activation(
                            rb, rr, AF.Exp, scale=-1.0), r=[FA(ro, 256)], w=[CB(rbo, 256)])

                    def w_norm(hp=hp, par=par, oo=oo):
                        for h2 in range(2):
                            rbo = cROWS + (par * 2 + h2) * TL
                            rb = cb[0:1, rbo:rbo + 256]
                            mm(6, psum[6][0:64, 0:256], ones_row[0:1, 0:64], rb,
                               True, True, r=[CB(rbo, 256), CB(cONES, 128)])
                            ibo = (L_IB, L_IB2)[h2]
                            ib = ba[h2 * 64:(h2 + 1) * 64, ibo:ibo + 256]
                            P.op("dve", lambda e, ib=ib: e.tensor_copy(
                                ib, psum[6][0:64, 0:256]), r=[PS(6)], w=[BA(ibo, 256)])
                            P.op("dve", lambda e, oo=oo, h2=h2, ib=ib: e.tensor_mul(
                                ba[h2 * 64:(h2 + 1) * 64, oo:oo + 256],
                                ba[h2 * 64:(h2 + 1) * 64, oo:oo + 256], ib),
                                r=[BA(ibo, 256), BA(oo, 256)], w=[BA(oo, 256)])
                    pend_w.append(w_norm)
                for fn_ in pend_w:
                    fn_()
                P.fence([BA(L_OW, 2048)])
                for m in range(NCH):
                    bank = m % 2
                    for k in range(NCH):
                        lhsT = ba[:, L_WO + k * 1024 + m * 128: L_WO + k * 1024 + (m + 1) * 128]
                        rhs = ba[:, L_OW + k * 256: L_OW + (k + 1) * 256]
                        mm(bank, psum[bank][:, 0:256], lhsT, rhs, k == 0, k == NCH - 1,
                           r=[BA(L_WO + k * 1024 + m * 128, 128), BA(L_OW + k * 256, 256)])
                    col = lo + qcol
                    tgt = FX(m)[:, col:col + 256]
                    P.op("dve", lambda e, tgt=tgt, bank=bank: e.tensor_add(
                        tgt, psum[bank][:, 0:256], tgt),
                        r=[PS(bank), FA(oXR + m * S + col, 256)],
                        w=[FA(oXR + m * S + col, 256)])
    def emit_ffn(l, n_local):
        emit_stash_bnd()
        for ks in range(NKS):
            w = 128 if ks < 21 else 42
            P.dma("pool", "w", lambda e, l=l, ks=ks, w=w: e.dma_start(
                out=ba[0:w, F_W2 + ks * 1024: F_W2 + ks * 1024 + 1024],
                in_=d_w2[l][ks * 128: ks * 128 + w, :]),
                w=[BA(F_W2 + ks * 1024, 1024)])
        for tp in range(2):
            tiles = [2 * tp, 2 * tp + 1]
            emit_ln(tiles, lambda ti, c: F_XN + c * 1024 + ti * TL, shifted=True,
                    layer=l, sq_off=F_SQ, mu_off=F_MU, isd_off=F_ISD)
            for j in range(NKS):
                w = 128 if j < 21 else 42
                p_ = j % 2
                bkA = (0, 1, 2, 3) if p_ == 0 else (4, 1, 5, 6)
                for half in range(2):
                    cofs = j * 128 + half * DI
                    ring = F_W1 + ((2 * j + half) % 4) * 1024
                    P.dma("pool", "w", lambda e, l=l, cofs=cofs, w=w, ring=ring:
                          e.dma_start(
                              out=ba[:, ring:ring + 8 * w].rearrange(
                                  "p (c m) -> p c m", c=8),
                              in_=d_w1[l][:, cofs:cofs + w].rearrange(
                                  "(c p) m -> p c m", p=128)),
                          w=[BA(ring, 1024)])
                    for k in range(NCH):
                        lhsT = ba[:, ring + k * w: ring + (k + 1) * w]
                        for t2 in range(2):
                            bank = bkA[half * 2 + t2]
                            rhs = ba[:, F_XN + k * 1024 + t2 * TL: F_XN + k * 1024 + (t2 + 1) * TL]
                            mm(bank, psum[bank][0:w, 0:TL], lhsT, rhs,
                               k == 0, k == NCH - 1,
                               r=[BA(ring + k * w, w), BA(F_XN + k * 1024 + t2 * TL, TL)],
                               ldw_key=("w1", j, half, k))
                for t2 in range(2):
                    w_ = w
                    hoff = F_H + t2 * 11264 + j * TL
                    gt = ba[0:w_, F_GT + p_ * TL:F_GT + p_ * TL + TL]
                    gb = bkA[2 + t2]
                    ab = bkA[t2]
                    P.op("act", lambda e, gt=gt, gb=gb, w_=w_, j=j: e.activation(
                        gt, psum[gb][0:w_, 0:TL], AF.Gelu,
                        bias=smcol(48 + j, w_)),
                        r=[PS(gb), SM_R], w=[BA(F_GT + p_ * TL, TL)])
                    P.op("dve", lambda e, hoff=hoff, gt=gt, ab=ab, w_=w_, j=j:
                         e.scalar_tensor_tensor(
                             ba[0:w_, hoff:hoff + TL], psum[ab][0:w_, 0:TL],
                             smcol(26 + j, w_), gt, ALU.add, ALU.mult),
                         r=[PS(ab), SM_R, BA(F_GT + p_ * TL, TL)], w=[BA(hoff, TL)])
            P.fence([BA(F_H, 22528)])
            for m in range(NCH):
                bks = ((2, 3), (4, 5))[m % 2]
                for ks in range(NKS):
                    w = 128 if ks < 21 else 42
                    lhsT = ba[0:w, F_W2 + ks * 1024 + m * 128: F_W2 + ks * 1024 + (m + 1) * 128]
                    for t2 in range(2):
                        rhs = ba[0:w, F_H + t2 * 11264 + ks * TL: F_H + t2 * 11264 + (ks + 1) * TL]
                        mm(bks[t2], psum[bks[t2]][:, 0:TL], lhsT, rhs,
                           ks == 0, ks == NKS - 1,
                           r=[BA(F_W2 + ks * 1024 + m * 128, 128),
                              BA(F_H + t2 * 11264 + ks * TL, TL)],
                           ldw_key=("w2", m, ks))
                for t2 in range(2):
                    col = (2 * tp + t2) * TL
                    tgt = FX(m)[:, col:col + TL]
                    P.op("dve", lambda e, tgt=tgt, bank=bks[t2], m=m: e.scalar_tensor_tensor(
                        tgt, psum[bank][:, 0:TL], smcol(18 + m), tgt, ALU.add, ALU.add),
                        r=[PS(bks[t2]), SM_R, FA(oXR + m * S + col, TL)],
                        w=[FA(oXR + m * S + col, TL)])
    def emit_stash():
        for c in range(NCH):
            P.dma("sp", "io", lambda e, c=c: e.dma_start(out=d_x1s[c], in_=FX(c)),
                  r=[FA(oXR + c * S, S)])

    def emit_final_add():
        for c in range(NCH):
            for piece in range(4):
                lo = piece * TL
                ro = (oR0, oR1)[(c * 4 + piece) % 2]
                st = fa[:, ro:ro + TL]
                P.dma("sp", "io", lambda e, c=c, lo=lo, st=st: e.dma_start(
                    out=st, in_=d_x1s[c][:, lo:lo + TL]), w=[FA(ro, TL)])
                P.op("dve", lambda e, c=c, lo=lo, st=st: e.tensor_add(
                    FX(c)[:, lo:lo + TL], FX(c)[:, lo:lo + TL], st),
                    r=[FA(ro, TL), FA(oXR + c * S + lo, TL)],
                    w=[FA(oXR + c * S + lo, TL)])

    tapn = [0]

    def emit_tap():
        if not n_taps or tapn[0] >= n_taps:
            return
        i = tapn[0]
        tapn[0] += 1
        for c in range(NCH):
            P.dma("sp", "io", lambda e, i=i, c=c: e.dma_start(
                out=d_dbg[i, c], in_=FX(c)), r=[FA(oXR + c * S, S)])

    final_toks = []
    for b in range(NB):
        emit_load_x(b)
        if cfg["with_main"]:
            emit_ln([0, 1, 2, 3], lambda ti, c: M_XN + c * S + ti * TL,
                    shifted=False, sq_off=M_SQ, mu_off=M_MU, isd_off=M_ISD)
            emit_main_attn()
        if cfg["with_slot"]:
            emit_stash()
            emit_ln([0, 1, 2, 3], lambda ti, c: E_XC + c * TL, shifted=False,
                    inplace=True, sq_off=E_SQ, mu_off=E_MU, isd_off=E_ISD)
            if cfg["n_local"] > 0:
                load_masks("local")
            for l in range(cfg["n_local"]):
                load_smalls(l)
                emit_local_attn(l)
                emit_tap()
                emit_ffn(l, cfg["n_local"])
                emit_tap()
            emit_final_add()
        final_toks += emit_store(b)

    P.op("sp", lambda e: None, waits=final_toks)

    P.finalize(nc, stack)
    nops = {e: len(P.ops[e]) for e in ENGS}
    nwaits = {e: sum(len(o.waits) for o in P.ops[e]) for e in ENGS}
    print("[kernel] op counts:", nops, "total", sum(nops.values()))
    print("[kernel] raw wait edges:", nwaits)

    with nc.Block() as blk:
        @blk.tensor
        def _(e):
            P.replay("pe", e)

        @blk.scalar
        def _(e):
            P.replay("act", e)

        @blk.vector
        def _(e):
            P.replay("dve", e)

        @blk.gpsimd
        def _(e):
            P.replay("pool", e)

        @blk.sync
        def _(e):
            P.replay("sp", e)

    stack.close()
    return nc


_PROG_CACHE = {}


def _get_program():
    key = tuple(sorted(_CFG.items()))
    if key not in _PROG_CACHE:
        _PROG_CACHE[key] = build_program(dict(_CFG))
    return _PROG_CACHE[key]


def kernel(L0_attn_x, L0_attn_WQ, L0_attn_WK, L0_attn_WV, L0_attn_WO,
           L0_ffn_Wqkv, L0_ffn_q_scale, L0_ffn_k_scale, L0_ffn_Wo,
           L0_ffn_ln_g, L0_ffn_ln_b, L0_ffn_W1, L0_ffn_b1, L0_ffn_W2,
           L0_ffn_b2):
    x = np.ascontiguousarray(np.asarray(L0_attn_x, np.float32))
    WQ = np.asarray(L0_attn_WQ, np.float32)
    WK = np.asarray(L0_attn_WK, np.float32)
    WV = np.asarray(L0_attn_WV, np.float32)
    WO = np.asarray(L0_attn_WO, np.float32)

    wqt = np.ascontiguousarray(WQ.reshape(8, 128, D).transpose(0, 2, 1))
    wkt = np.ascontiguousarray(WK.reshape(8, 128, D).transpose(0, 2, 1))
    wvt = np.ascontiguousarray(WV.reshape(8, 128, D).transpose(0, 2, 1))
    wot = np.ascontiguousarray(WO.transpose(0, 2, 1).reshape(D, D))

    qs = np.asarray(L0_ffn_q_scale, np.float32)
    ks = np.asarray(L0_ffn_k_scale, np.float32)
    b1 = np.asarray(L0_ffn_b1, np.float32)
    b2 = np.asarray(L0_ffn_b2, np.float32)
    lng = np.asarray(L0_ffn_ln_g, np.float32)
    lnb = np.asarray(L0_ffn_ln_b, np.float32)

    sm = np.zeros((6, 128, 72), np.float32)
    qs2 = np.concatenate([qs, qs], axis=1)
    ks2 = np.concatenate([ks, ks], axis=1)
    sm[:, :, 0] = qs2
    sm[:, :, 1] = ks2
    sm[:, :, 2:10] = _cols(lng, 8)
    sm[:, :, 10:18] = _cols(lnb, 8)
    sm[:, :, 18:26] = _cols(b2, 8)
    sm[:, :, 26:48] = _cols(b1[:, :DI], NKS)
    sm[:, :, 48:70] = _cols(b1[:, DI:], NKS)

    shared = {
        "wqt": wqt, "wkt": wkt, "wvt": wvt, "wot": wot,
        "wqkv": np.ascontiguousarray(np.asarray(L0_ffn_Wqkv, np.float32)),
        "wol": np.ascontiguousarray(np.asarray(L0_ffn_Wo, np.float32)),
        "w1": np.ascontiguousarray(np.asarray(L0_ffn_W1, np.float32)),
        "w2": np.ascontiguousarray(np.asarray(L0_ffn_W2, np.float32)),
        "sm": sm,
        "tab": _rotary_tables(),
        "mloc": _local_masks(),
        "mmain": _main_masks(),
        "ident": _bf(np.eye(128, dtype=np.float32)),
        "bd2": _bf(np.kron(np.eye(2), np.ones((64, 1)))),
        "bd2t": _bf(np.kron(np.eye(2), np.ones((1, 64)))),
        "onesr": _bf(np.ones((1, 512), np.float32)),
    }

    in_maps = []
    for c in range(NCORES):
        shard = x[2 * c:2 * c + 2]
        xs = np.ascontiguousarray(
            shard.transpose(0, 2, 1).reshape(NB, NCH, 128, S))
        in_maps.append({"x": xs, **shared})

    nc = _get_program()
    res = run_bass_kernel_spmd(nc, in_maps, core_ids=list(range(NCORES)))
    outs = []
    for c in range(NCORES):
        o = res.results[c]["out"]
        outs.append(np.asarray(o, np.float32).reshape(NB, D, S).transpose(0, 2, 1))
    if _CFG.get("taps"):
        global _TAPS
        _TAPS = [np.asarray(res.results[c]["dbg"], np.float32) for c in range(NCORES)]
    return np.ascontiguousarray(np.concatenate(outs, axis=0))


# version 09 (final)
# speedup vs baseline: 1.2057x; 1.2057x over previous
import contextlib
import numpy as np
import ml_dtypes
import concourse.bass as bass
import concourse.mybir as mybir
from concourse.bass_utils import run_bass_kernel_spmd

F32 = mybir.dt.float32
BF16 = mybir.dt.bfloat16
AF = mybir.ActivationFunctionType
ALU = mybir.AluOpType

NCORES = 8
NB = 2
S = 2048
D = 1024
NCH = 8
H = 16
DH = 64
WIN = 256
DI = 2730
NKS = 22
TL = 512
NTL = 4
NEG = -1.0e9

_CFG = {"with_main": True, "with_slot": True, "n_local": 6}

ENGS = ("pe", "act", "dve", "pool", "sp")


class _Op:
    __slots__ = ("fn", "eng", "waits", "inc_flag", "dma_stream", "ms", "seq")

    def __init__(self, eng, fn):
        self.eng = eng
        self.fn = fn
        self.waits = []
        self.inc_flag = False
        self.dma_stream = None
        self.ms = None
        self.seq = 0


class _Tok:
    __slots__ = ("kind", "eng", "op", "stream", "val")

    def __init__(self, kind, eng, op=None, stream=None, val=None):
        self.kind = kind
        self.eng = eng
        self.op = op
        self.stream = stream
        self.val = val


class _Buf:
    __slots__ = ("lo", "hi", "writer", "readers")

    def __init__(self, lo, hi, writer=None, readers=None):
        self.lo = lo
        self.hi = hi
        self.writer = writer
        self.readers = readers if readers is not None else []


class Prog:
    def __init__(self):
        self.ops = {e: [] for e in ENGS}
        self.dma_counts = {}
        self.live = {}
        self.psum_pending = {}
        self.seq = 0

    def _need(self, op, tok):
        if tok.kind == "op":
            if tok.op.eng == op.eng:
                return
            tok.op.inc_flag = True
            op.waits.append(tok)
            return
        op.waits.append(_Tok("dma", tok.eng, stream=tok.stream,
                             val=self.dma_counts[tok.stream]))

    def _acc(self, tok, op, reads, writes):
        for (key, lo, hi) in reads:
            for b in self.live.get(key, ()):
                if b.lo < hi and lo < b.hi:
                    if b.writer is not None:
                        self._need(op, b.writer)
                    b.readers.append(tok)
        for (key, lo, hi) in writes:
            lst = self.live.setdefault(key, [])
            keep, deps = [], []
            for b in lst:
                if b.lo < hi and lo < b.hi:
                    if b.writer is not None:
                        deps.append(b.writer)
                    deps.extend(b.readers)
                    if b.lo < lo:
                        keep.append(_Buf(b.lo, lo, b.writer, list(b.readers)))
                    if b.hi > hi:
                        keep.append(_Buf(hi, b.hi, b.writer, list(b.readers)))
                else:
                    keep.append(b)
            for t in deps:
                if t is tok:
                    continue
                self._need(op, t)
            keep.append(_Buf(lo, hi, tok, []))
            self.live[key] = keep

    def op(self, eng, fn, r=(), w=(), waits=()):
        o = _Op(eng, fn)
        self.seq += 1
        o.seq = self.seq
        for t in waits:
            self._need(o, t)
        tok = _Tok("op", eng, op=o)
        self._acc(tok, o, r, w)
        self.ops[eng].append(o)
        return tok

    def dma(self, eng, stream, fn, r=(), w=(), waits=()):
        o = _Op(eng, fn)
        self.seq += 1
        o.seq = self.seq
        o.dma_stream = stream
        self.dma_counts.setdefault(stream, 0)
        for t in waits:
            self._need(o, t)
        tok = _Tok("dma", eng, stream=stream, val=None)
        self._acc(tok, o, r, w)
        self.dma_counts[stream] += 16
        self.ops[eng].append(o)
        return tok

    def fence(self, keys):
        latest = {}
        bufs = []
        for (key, lo, hi) in keys:
            for b in self.live.get(key, ()):
                if b.lo < hi and lo < b.hi and b.writer is not None:
                    t = b.writer
                    g = ("op", t.eng) if t.kind == "op" else ("dma", t.stream)
                    cur = latest.get(g)
                    if t.kind == "op":
                        if cur is None or t.op.seq > cur.op.seq:
                            latest[g] = t
                    else:
                        if cur is None or (t.val or self.dma_counts[t.stream]) > \
                                (cur.val or self.dma_counts[cur.stream]):
                            latest[g] = t
                    bufs.append((b, g))
        for b, g in bufs:
            b.writer = latest[g]

    def group_start(self, bank):
        assert not self.psum_pending.get(bank), f"bank {bank} pending"
        self.psum_pending[bank] = True

    def group_stop(self, bank):
        assert self.psum_pending.get(bank), f"bank {bank} not started"
        self.psum_pending[bank] = False

    def finalize(self, nc, stack):
        self.psem = {}
        self.dsem = {}
        for e in ENGS:
            if any(o.inc_flag for o in self.ops[e]):
                self.psem[e] = stack.enter_context(nc.semaphore(f"pg_{e}"))
        for s in self.dma_counts:
            self.dsem[s] = stack.enter_context(nc.semaphore(f"dm_{s}"))
        for e in ENGS:
            k = 0
            for o in self.ops[e]:
                if o.inc_flag:
                    k += 1
                    o.ms = k

    def replay(self, engname, eng):
        waited = {}
        for o in self.ops[engname]:
            for t in o.waits:
                if t.kind == "op":
                    sem, val = self.psem[t.eng], t.op.ms
                else:
                    sem, val = self.dsem[t.stream], t.val
                if waited.get(id(sem), -1) < val:
                    eng.wait_ge(sem, val)
                    waited[id(sem)] = val
            inst = o.fn(eng)
            if o.dma_stream is not None:
                inst.then_inc(self.dsem[o.dma_stream], 16)
            elif o.inc_flag:
                assert inst is not None
                inst.then_inc(self.psem[engname], 1)


def _bf(x):
    return np.asarray(x).astype(ml_dtypes.bfloat16)


def _rotary_tables():
    half = np.arange(0, DH, 2, dtype=np.float64)
    inv_freq = 1.0 / 10000.0 ** (half / DH)
    t = np.arange(S, dtype=np.float64)
    freqs = t[:, None] * inv_freq[None, :]
    freqs = np.concatenate([freqs, freqs], -1)
    base_scale = (half + 0.4 * DH) / (1.4 * DH)
    power = (t - S // 2) / (WIN // 2)
    scale = base_scale[None, :] ** power[:, None]
    scale = np.concatenate([scale, scale], -1)
    cos, sin = np.cos(freqs), np.sin(freqs)
    inv = 1.0 / scale
    sign = np.where(np.arange(64) < 32, -1.0, 1.0)[None, :]
    tabs = [cos * scale, sin * scale * sign, cos * inv, sin * inv * sign]
    out = np.zeros((4, 128, S), np.float32)
    for i, tb in enumerate(tabs):
        fm = tb.T.astype(np.float32)
        out[i, :64] = fm
        out[i, 64:] = fm
    return _bf(out)


def _local_masks():
    m = np.zeros((4, 128, 256), np.float32)
    p = np.arange(128)[:, None]
    n = np.arange(256)[None, :]
    keeps = [
        (n < 128) & (p >= n),
        (n < 128) | (p >= n - 128),
        (n >= 128) | (p <= n),
        (n >= 128) & (p <= n - 128),
    ]
    for i, k in enumerate(keeps):
        m[i] = np.where(k, 0.0, NEG)
    return _bf(m)


def _main_masks():
    m = np.zeros((4, 128, 512), np.float32)
    p = np.arange(128)[:, None]
    n = np.arange(512)[None, :]
    for r in range(4):
        m[r] = np.where(p <= n - 128 * r, 0.0, NEG)
    return _bf(m)


def _cols(vec, width):
    L, n = np.asarray(vec).shape
    v = np.pad(np.asarray(vec, np.float32), ((0, 0), (0, width * 128 - n)))
    return np.ascontiguousarray(v.reshape(L, width, 128).transpose(0, 2, 1))


def build_program(cfg):
    nc = bass.Bass("TRN2", target_bir_lowering=False)
    stack = contextlib.ExitStack()

    dram = {}

    def din(name, shape, dt=F32, kind="ExternalInput"):
        dram[name] = nc.dram_tensor(name, shape, dt, kind=kind).ap()
        return dram[name]

    d_x = din("x", [NB, NCH, 128, S])
    d_out = din("out", [NB, NCH, 128, S], kind="ExternalOutput")
    d_wqt = din("wqt", [8, D, 128])
    d_wkt = din("wkt", [8, D, 128])
    d_wvt = din("wvt", [8, D, 128])
    d_wot = din("wot", [D, D])
    d_wqkv = din("wqkv", [6, D, 3 * D])
    d_wol = din("wol", [6, D, D])
    d_w1 = din("w1", [6, D, 2 * DI])
    d_w2 = din("w2", [6, DI, D])
    d_sm = din("sm", [6, 128, 72])
    d_tab = din("tab", [4, 128, S], BF16)
    d_mloc = din("mloc", [4, 128, 256], BF16)
    d_mmain = din("mmain", [4, 128, 512], BF16)
    d_ident = din("ident", [128, 128], BF16)
    d_bd2 = din("bd2", [128, 2], BF16)
    d_bd2t = din("bd2t", [2, 128], BF16)
    d_ones = din("onesr", [1, 512], BF16)
    d_qkrow = din("qkrow", [6, 2, 2, 128], F32)
    d_x1s = din("x1scr", [NCH, 128, S], F32, kind="Internal")
    n_taps = int(cfg.get("taps", 0))
    d_dbg = din("dbg", [max(n_taps, 1), NCH, 128, S], F32, kind="ExternalOutput") \
        if n_taps else None

    A_F = 17500
    A_B = 65152
    A_C = 5200
    fa = stack.enter_context(nc.sbuf_tensor("fa", [128, A_F], F32))
    ba = stack.enter_context(nc.sbuf_tensor("ba", [128, A_B], BF16))
    cb = stack.enter_context(nc.sbuf_tensor("cb", [128, A_C], BF16))
    psum = [stack.enter_context(nc.psum_tensor(f"pb{i}", [128, 512], F32))
            for i in range(7)]
    psbf = stack.enter_context(nc.psum_tensor("pb7", [128, 1024], BF16))

    oXR = 0
    oR0 = 16384
    oR1 = 16896
    oSM = 17408
    oZC = 17408 + 72
    oEPS = oZC + 1
    oBND = oEPS + 1
    assert oBND + 12 <= A_F

    cID, cBD2, cBD2T, cONES, cOC = 0, 128, 130, 258, 770
    cMSK = 771
    cROWS = 2819
    cBDQ = cROWS + 2048
    cBDK = cBDQ + 128
    assert cBDK + 128 <= A_C

    L_WQKV = 0
    L_WO = 24576
    L_Q = 32768
    L_KR = 36864
    L_VN = 45056
    L_XS = 53376
    L_TAB = 57472
    L_OW = 59520
    L_PT = 61568
    L_SC = 62592
    L_VS = 63616
    L_IB = 64128
    L_IB2 = 64640
    assert L_IB2 + 512 <= A_B
    M_XN = 0
    M_OALL = 16384
    M_QP = 32768
    M_KP = 34816
    M_VN = 36864
    M_W = 38944
    M_WOT = 45088
    M_PT = 53280
    M_VS = 55328
    M_SQ = 55840
    M_MU = 56352
    M_ISD = 56864
    assert M_ISD + 512 <= A_B
    F_XN = 0
    F_H = 8192
    F_W2 = 30720
    F_W1 = 53248
    F_GT = 57344
    F_SQ = 58368
    F_MU = 58880
    F_ISD = 59392
    E_XC = 0
    E_SQ = 4096
    E_MU = 4608
    E_ISD = 5120

    P = Prog()

    def BA(lo, n): return ("ba", lo, lo + n)
    def FA(lo, n): return ("fa", lo, lo + n)
    def CB(lo, n): return ("cb", lo, lo + n)
    def PS(i): return ("ps", i, i + 1)

    def FX(c):
        return fa[:, oXR + c * S: oXR + (c + 1) * S]

    ident = cb[:, cID:cID + 128]
    bd2 = cb[:, cBD2:cBD2 + 2]
    bd2t = cb[0:2, cBD2T:cBD2T + 128]
    ones_row = cb[0:1, cONES:cONES + 512]
    onescol = cb[:, cOC:cOC + 1]

    P.last_ldw = None

    def mm(bank, out_ap, lhsT, rhs, start, stop, r, tp=None, ldw_key=None):
        if start:
            P.group_start(bank)
        if stop:
            P.group_stop(bank)
        key = (ldw_key, tp) if ldw_key is not None else None
        if key is None or key != P.last_ldw:
            P.op("pe", lambda e, l=lhsT, tp=tp: e.ldweights(l, tile_position=tp),
                 r=list(r))
        P.last_ldw = key
        reads = list(r)
        writes = []
        if start or stop:
            writes = [PS(bank)]
        if not start:
            reads = reads + [PS(bank)]
        return P.op("pe",
                    lambda e, o=out_ap, l=lhsT, rh=rhs, st=start, sp=stop, tp=tp:
                    e.matmul(o, l, rh, start=st, stop=sp, tile_position=tp),
                    r=reads, w=writes)

    P.dma("sp", "io", lambda e: e.dma_start(out=ident, in_=d_ident), w=[CB(cID, 128)])
    P.dma("sp", "io", lambda e: e.dma_start(out=bd2, in_=d_bd2), w=[CB(cBD2, 2)])
    P.dma("sp", "io", lambda e: e.dma_start(out=bd2t, in_=d_bd2t), w=[CB(cBD2T, 128)])
    P.dma("sp", "io", lambda e: e.dma_start(out=ones_row, in_=d_ones), w=[CB(cONES, 512)])
    P.op("dve", lambda e: e.memset(onescol, 1.0), w=[CB(cOC, 1)])
    P.op("dve", lambda e: e.memset(fa[:, oZC:oZC + 1], 0.0), w=[FA(oZC, 1)])
    P.op("dve", lambda e: e.memset(fa[:, oEPS:oEPS + 1], 1e-5), w=[FA(oEPS, 1)])

    def load_masks(which):
        src = d_mmain if which == "main" else d_mloc
        wd = 512 if which == "main" else 256
        for i in range(4):
            P.dma("sp", "io",
                  lambda e, i=i, src=src, wd=wd: e.dma_start(
                      out=cb[:, cMSK + i * wd: cMSK + (i + 1) * wd], in_=src[i]),
                  w=[CB(cMSK + i * wd, wd)] if i else [CB(cMSK, 2048)])

    def mask_ap(which, i):
        wd = 512 if which == "main" else 256
        return cb[:, cMSK + i * wd: cMSK + i * wd + wd], CB(cMSK + i * wd, wd)

    def preset_local_ones():
        for h in range(H):
            for s_ in range(8):
                off = L_VN + h * 520 + s_ * 65 + 64
                P.op("dve", lambda e, off=off: e.memset(ba[:, off:off + 1], 1.0),
                     w=[BA(off, 1)])

    def preset_main_ones():
        for h in range(2):
            for s_ in range(16):
                off = M_VN + h * 1040 + s_ * 65 + 64
                P.op("dve", lambda e, off=off: e.memset(ba[:, off:off + 1], 1.0),
                     w=[BA(off, 1)])

    def load_smalls(l):
        P.dma("sp", "io", lambda e, l=l: e.dma_start(
            out=fa[:, oSM:oSM + 72], in_=d_sm[l]), w=[FA(oSM, 72)])
        P.dma("sp", "io", lambda e, l=l: e.dma_start(
            out=fa[0:2, oR0:oR0 + 256].rearrange("p (a b) -> p a b", a=2),
            in_=d_qkrow[l].rearrange("a p b -> p a b")), w=[FA(oR0, 256)])
        P.op("dve", lambda e: e.tensor_mul(
            cb[0:2, cBDQ:cBDQ + 128], bd2t, fa[0:2, oR0:oR0 + 128]),
            r=[CB(cBD2T, 128), FA(oR0, 128)], w=[CB(cBDQ, 128)])
        P.op("dve", lambda e: e.tensor_mul(
            cb[0:2, cBDK:cBDK + 128], bd2t, fa[0:2, oR0 + 128:oR0 + 256]),
            r=[CB(cBD2T, 128), FA(oR0 + 128, 128)], w=[CB(cBDK, 128)])

    def smcol(i, parts=128):
        return fa[0:parts, oSM + i:oSM + i + 1]

    SM_R = FA(oSM, 72)

    def shift_view(c, lo, n, shifted):
        if (not shifted) or c < 4:
            return FX(c)[:, lo:lo + n], FA(oXR + c * S + lo, n), None, None
        main = FX(c)[:, lo:lo + n - 1]
        mkey = FA(oXR + c * S + lo, n - 1)
        if lo == 0:
            return main, mkey, fa[:, oZC:oZC + 1], FA(oZC, 1)
        cell = oBND + (c - 4) * 3 + (lo // TL - 1)
        return main, mkey, fa[:, cell:cell + 1], FA(cell, 1)

    def emit_stash_bnd():
        for c in range(4, NCH):
            for bi, col in enumerate((511, 1023, 1535)):
                cell = oBND + (c - 4) * 3 + bi
                P.op("dve", lambda e, c=c, col=col, cell=cell: e.tensor_copy(
                    fa[:, cell:cell + 1], FX(c)[:, col:col + 1]),
                    r=[FA(oXR + c * S + col, 1)], w=[FA(cell, 1)])

    def emit_ln(tiles, dst, shifted, layer=None, inplace=False,
                sq_off=None, mu_off=None, isd_off=None):
        for ti, t in enumerate(tiles):
            lo = t * TL
            for c in range(NCH):
                dlo = dst(ti, c)
                d = ba[:, dlo:dlo + TL]
                sv, rk, bnd, bk = shift_view(c, lo, TL, shifted)
                if bnd is not None:
                    P.op("dve", lambda e, d=d, s=sv: e.tensor_copy(d[:, 1:TL], s),
                         r=[rk], w=[BA(dlo + 1, TL - 1)])
                    P.op("dve", lambda e, d=d, b_=bnd: e.tensor_copy(d[:, 0:1], b_),
                         r=[bk], w=[BA(dlo, 1)])
                else:
                    P.op("dve", lambda e, d=d, s=sv: e.tensor_copy(d, s),
                         r=[rk], w=[BA(dlo, TL)])
            for c in range(NCH):
                dlo = dst(ti, c)
                d = ba[:, dlo:dlo + TL]
                sq = ba[:, sq_off:sq_off + TL]
                P.op("dve", lambda e, d=d, q=sq: e.tensor_mul(q, d, d),
                     r=[BA(dlo, TL)], w=[BA(sq_off, TL)])
                mm(5, psum[5][0:1, 0:TL], onescol, d, c == 0, c == NCH - 1,
                   r=[BA(dlo, TL), CB(cOC, 1)])
                mm(6, psum[6][0:1, 0:TL], onescol, sq, c == 0, c == NCH - 1,
                   r=[BA(sq_off, TL), CB(cOC, 1)])
            mu = fa[0:1, oR0:oR0 + TL]
            e2 = fa[0:1, oR1:oR1 + TL]
            P.op("dve", lambda e, m=mu: e.tensor_scalar_mul(m, psum[5][0:1, 0:TL], 1.0 / D),
                 r=[PS(5)], w=[FA(oR0, TL)])
            P.op("dve", lambda e, v=e2: e.tensor_scalar_mul(v, psum[6][0:1, 0:TL], 1.0 / D),
                 r=[PS(6)], w=[FA(oR1, TL)])
            mu_bf = cb[0:1, cROWS:cROWS + TL]
            P.op("act", lambda e, m=mu, mb=mu_bf: e.activation(mb, m, AF.Identity),
                 r=[FA(oR0, TL)], w=[CB(cROWS, TL)])
            P.op("dve", lambda e, m=mu: e.scalar_tensor_tensor(
                m, m, -1.0, m, ALU.mult, ALU.mult), r=[FA(oR0, TL)], w=[FA(oR0, TL)])
            P.op("dve", lambda e, v=e2, m=mu: e.tensor_add(v, v, m),
                 r=[FA(oR0, TL), FA(oR1, TL)], w=[FA(oR1, TL)])
            P.op("act", lambda e, v=e2: e.activation(
                v, v, AF.Ln, bias=fa[0:1, oEPS:oEPS + 1]),
                r=[FA(oR1, TL), FA(oEPS, 1)], w=[FA(oR1, TL)])
            isd_bf = cb[0:1, cROWS + TL:cROWS + 2 * TL]
            P.op("act", lambda e, v=e2, ib=isd_bf: e.activation(ib, v, AF.Exp, scale=-0.5),
                 r=[FA(oR1, TL)], w=[CB(cROWS + TL, TL)])
            mm(5, psum[5][:, 0:TL], ones_row[0:1, 0:128], mu_bf, True, True,
               r=[CB(cROWS, TL), CB(cONES, 128)])
            muB = ba[:, mu_off:mu_off + TL]
            P.op("dve", lambda e, m=muB: e.tensor_copy(m, psum[5][:, 0:TL]),
                 r=[PS(5)], w=[BA(mu_off, TL)])
            mm(6, psum[6][:, 0:TL], ones_row[0:1, 0:128], isd_bf, True, True,
               r=[CB(cROWS + TL, TL), CB(cONES, 128)])
            isdB = ba[:, isd_off:isd_off + TL]
            P.op("dve", lambda e, i_=isdB: e.tensor_copy(i_, psum[6][:, 0:TL]),
                 r=[PS(6)], w=[BA(isd_off, TL)])
            for c in range(NCH):
                dlo = dst(ti, c)
                d = ba[:, dlo:dlo + TL]
                if inplace:
                    tgt = FX(c)[:, lo:lo + TL]
                    P.op("dve", lambda e, t_=tgt, m=muB: e.tensor_sub(t_, t_, m),
                         r=[BA(mu_off, TL), FA(oXR + c * S + lo, TL)],
                         w=[FA(oXR + c * S + lo, TL)])
                    P.op("dve", lambda e, t_=tgt, i_=isdB: e.tensor_mul(t_, t_, i_),
                         r=[BA(isd_off, TL), FA(oXR + c * S + lo, TL)],
                         w=[FA(oXR + c * S + lo, TL)])
                else:
                    sv, rk, bnd, bk = shift_view(c, lo, TL, shifted)
                    if bnd is not None:
                        P.op("dve", lambda e, d=d, s=sv, m=muB: e.tensor_sub(
                            d[:, 1:TL], s, m[:, 1:TL]),
                            r=[rk, BA(mu_off, TL)], w=[BA(dlo, TL)])
                        P.op("dve", lambda e, d=d, b_=bnd, m=muB: e.tensor_sub(
                            d[:, 0:1], b_, m[:, 0:1]),
                            r=[bk, BA(mu_off, TL)], w=[BA(dlo, 1)])
                    else:
                        P.op("dve", lambda e, d=d, s=sv, m=muB: e.tensor_sub(d, s, m),
                             r=[rk, BA(mu_off, TL)], w=[BA(dlo, TL)])
                    P.op("dve", lambda e, d=d, i_=isdB: e.tensor_mul(d, d, i_),
                         r=[BA(isd_off, TL)], w=[BA(dlo, TL)])
                    if layer is not None:
                        P.op("act", lambda e, d=d, c=c, l=layer: e.activation(
                            d, d, AF.Identity, scale=smcol(2 + c), bias=smcol(10 + c)),
                            r=[SM_R], w=[BA(dlo, TL)])
        if inplace:
            P.fence([FA(oXR, NCH * S)])
        else:
            w0 = dst(0, 0)
            wN = dst(len(list(tiles)) - 1, NCH - 1) + TL
            P.fence([BA(w0, wN - w0)])

    def emit_load_x(b):
        for c in range(NCH):
            P.dma("sp", "io", lambda e, b=b, c=c: e.dma_start(
                out=FX(c), in_=d_x[b, c]), w=[FA(oXR + c * S, S)])

    def emit_store(b):
        toks = []
        for c in range(NCH):
            toks.append(P.dma("sp", "io", lambda e, b=b, c=c: e.dma_start(
                out=d_out[b, c], in_=FX(c)), r=[FA(oXR + c * S, S)]))
        return toks

    def emit_main_attn():
        load_masks("main")
        preset_main_ones()
        P.dma("pool", "w", lambda e: e.dma_start(
            out=ba[:, M_WOT:M_WOT + 8192].rearrange("p (c m) -> p c m", c=8),
            in_=d_wot.rearrange("(c p) m -> p c m", p=128)),
            w=[BA(M_WOT, 8192)])
        for hp in range(8):
            wb = M_W + (hp % 2) * 3072
            for wi, dsrc in enumerate((d_wqt, d_wkt, d_wvt)):
                P.dma("pool", "w", lambda e, hp=hp, wb=wb, wi=wi, dsrc=dsrc:
                      e.dma_start(
                          out=ba[:, wb + wi * 1024: wb + (wi + 1) * 1024]
                          .rearrange("p (c m) -> p c m", c=8),
                          in_=dsrc[hp].rearrange("(c p) m -> p c m", p=128)),
                      w=[BA(wb + wi * 1024, 1024)])
            for wi, dst_off in enumerate((M_QP, M_KP, None)):
                for k in range(NCH):
                    lhsT = ba[:, wb + wi * 1024 + k * 128: wb + wi * 1024 + (k + 1) * 128]
                    for qt in range(NTL):
                        rhs = ba[:, M_XN + k * S + qt * TL: M_XN + k * S + qt * TL + TL]
                        mm(qt, psum[qt][:, 0:TL], lhsT, rhs, k == 0, k == NCH - 1,
                           r=[BA(wb + wi * 1024 + k * 128, 128),
                              BA(M_XN + k * S + qt * TL, TL)],
                           ldw_key=("mproj", hp, wi, k))
                for qt in range(NTL):
                    bank = qt
                    if dst_off is not None:
                        dl = dst_off + qt * TL
                        P.op("act", lambda e, dl=dl, bank=bank: e.activation(
                            ba[:, dl:dl + TL], psum[bank][:, 0:TL], AF.Identity),
                            r=[PS(bank)], w=[BA(dl, TL)])
                    else:
                        P.op("act", lambda e, bank=bank: e.activation(
                            ba[:, M_VS:M_VS + TL], psum[bank][:, 0:TL], AF.Identity),
                            r=[PS(bank)], w=[BA(M_VS, TL)])
                        P.last_ldw = None
                        for cb4 in range(4):
                            kb = qt * 4 + cb4
                            pslot = (cb4 % 2) * 128
                            P.op("pe", lambda e, cb4=cb4, pslot=pslot:
                                 e.transpose(psbf[:, pslot:pslot + 128],
                                             ba[:, M_VS + cb4 * 128:M_VS + (cb4 + 1) * 128],
                                             ident),
                                 r=[BA(M_VS, TL), CB(cID, 128)], w=[PS(7)])
                            for h2 in range(2):
                                off = M_VN + h2 * 1040 + kb * 65
                                P.op("dve", lambda e, off=off, pslot=pslot, h2=h2:
                                     e.tensor_copy(ba[:, off:off + 64],
                                                   psbf[:, pslot + h2 * 64:pslot + h2 * 64 + 64]),
                                     r=[PS(7)], w=[BA(off, 64)])
            P.fence([BA(M_QP, 2048), BA(M_KP, 2048), BA(M_VN, 2080)])
            for qt in range(NTL):
                nkb = qt * 4 + 4

                def st_mm(kb):
                    bA = kb % 2
                    bB = 2 + kb % 2
                    masked = kb >= qt * 4
                    mm(bA, psum[bA][:, 0:TL],
                       ba[0:64, M_KP + kb * 128: M_KP + (kb + 1) * 128],
                       ba[0:64, M_QP + qt * TL: M_QP + qt * TL + TL],
                       True, not masked,
                       r=[BA(M_KP + kb * 128, 128), BA(M_QP + qt * TL, TL)])
                    mm(bB, psum[bB][:, 0:TL],
                       ba[64:128, M_KP + kb * 128: M_KP + (kb + 1) * 128],
                       ba[64:128, M_QP + qt * TL: M_QP + qt * TL + TL],
                       True, not masked, tp=(64, 0),
                       r=[BA(M_KP + kb * 128, 128), BA(M_QP + qt * TL, TL)])
                    if masked:
                        mk, mkr = mask_ap("main", kb - qt * 4)
                        mm(bA, psum[bA][:, 0:TL], ident, mk, False, True,
                           r=[CB(cID, 128), mkr], ldw_key="ident")
                        mm(bB, psum[bB][:, 0:TL], ident, mk, False, True,
                           r=[CB(cID, 128), mkr], ldw_key="ident")
                    pA = M_PT + (kb % 2) * TL
                    pB = M_PT + 1024 + (kb % 2) * TL
                    P.op("act", lambda e, pA=pA, bA=bA: e.activation(
                        ba[:, pA:pA + TL], psum[bA][:, 0:TL], AF.Exp, scale=0.125),
                        r=[PS(bA)], w=[BA(pA, TL)])
                    P.op("act", lambda e, pB=pB, bB=bB: e.activation(
                        ba[:, pB:pB + TL], psum[bB][:, 0:TL], AF.Exp, scale=0.125),
                        r=[PS(bB)], w=[BA(pB, TL)])

                def pv_mm(kb):
                    pA = M_PT + (kb % 2) * TL
                    pB = M_PT + 1024 + (kb % 2) * TL
                    for h2, (pof, ob) in enumerate(((pA, 4), (pB, 5))):
                        vno = M_VN + h2 * 1040 + kb * 65
                        mm(ob, psum[ob][0:65, 0:TL], ba[:, vno:vno + 65],
                           ba[:, pof:pof + TL], kb == 0, kb == nkb - 1,
                           r=[BA(vno, 65), BA(pof, TL)])

                st_mm(0)
                for kb in range(nkb):
                    if kb + 1 < nkb:
                        st_mm(kb + 1)
                    pv_mm(kb)
                par = qt % 2
                oo = M_OALL + hp * S + qt * TL
                for h2, ob in enumerate((4, 5)):
                    P.op("dve", lambda e, oo=oo, h2=h2, ob=ob: e.tensor_copy(
                        ba[h2 * 64:(h2 + 1) * 64, oo:oo + TL],
                        psum[ob][0:64, 0:TL]),
                        r=[PS(ob)], w=[BA(oo, TL)])
                    ro = (oR0, oR1)[par]
                    rr = fa[0:1, ro:ro + TL]
                    P.op("act", lambda e, ob=ob, rr=rr: e.activation(
                        rr, psum[ob][64:65, 0:TL], AF.Ln), r=[PS(ob)], w=[FA(ro, TL)])
                    rbo = cROWS + (par * 2 + h2) * TL
                    rb = cb[0:1, rbo:rbo + TL]
                    P.op("act", lambda e, rr=rr, rb=rb: e.activation(
                        rb, rr, AF.Exp, scale=-1.0), r=[FA(ro, TL)], w=[CB(rbo, TL)])
                    mm(6, psum[6][0:64, 0:TL], ones_row[0:1, 0:64], rb, True, True,
                       r=[CB(rbo, TL), CB(cONES, 128)])
                    ibo = (M_SQ, M_MU)[h2]
                    ib = ba[h2 * 64:(h2 + 1) * 64, ibo:ibo + TL]
                    P.op("dve", lambda e, ib=ib, h2=h2: e.tensor_copy(
                        ib, psum[6][0:64, 0:TL]), r=[PS(6)], w=[BA(ibo, TL)])
                    P.op("dve", lambda e, oo=oo, h2=h2, ib=ib: e.tensor_mul(
                        ba[h2 * 64:(h2 + 1) * 64, oo:oo + TL],
                        ba[h2 * 64:(h2 + 1) * 64, oo:oo + TL], ib),
                        r=[BA(ibo, TL), BA(oo, TL)], w=[BA(oo, TL)])
        P.fence([BA(M_OALL, 16384)])
        for m in range(NCH):
            for k in range(NCH):
                lhsT = ba[:, M_WOT + k * 1024 + m * 128: M_WOT + k * 1024 + (m + 1) * 128]
                for qt in range(NTL):
                    rhs = ba[:, M_OALL + k * S + qt * TL: M_OALL + k * S + qt * TL + TL]
                    mm(qt, psum[qt][:, 0:TL], lhsT, rhs, k == 0, k == NCH - 1,
                       r=[BA(M_WOT + k * 1024 + m * 128, 128),
                          BA(M_OALL + k * S + qt * TL, TL)],
                       ldw_key=("mwo", m, k))
            for qt in range(NTL):
                tgt = FX(m)[:, qt * TL:(qt + 1) * TL]
                P.op("dve", lambda e, tgt=tgt, qt=qt: e.tensor_add(
                    tgt, psum[qt][:, 0:TL], tgt),
                    r=[PS(qt), FA(oXR + m * S + qt * TL, TL)],
                    w=[FA(oXR + m * S + qt * TL, TL)])
    def emit_local_attn(l, prefetched=False):
        preset_local_ones()
        emit_stash_bnd()
        prev = None
        for qd in range(4):
            mlo = qd * 6
            wkeys = [("ba", L_WQKV + c * 3072 + mlo * 128,
                      L_WQKV + c * 3072 + (mlo + 6) * 128) for c in range(NCH)]
            tok = P.dma("pool", f"wq{qd}",
                        lambda e, l=l, mlo=mlo: e.dma_start(
                            out=ba[:, L_WQKV + 0:L_WQKV + 24576]
                            .rearrange("p (c m) -> p c m", c=8)[:, :, mlo * 128:(mlo + 6) * 128],
                            in_=d_wqkv[l].rearrange("(c p) m -> p c m", p=128)
                            [:, :, mlo * 128:(mlo + 6) * 128]),
                        w=wkeys,
                        waits=[prev] if prev is not None else ())
            prev = tok
        P.dma("pool", "wo", lambda e, l=l: e.dma_start(
            out=ba[:, L_WO:L_WO + 8192].rearrange("p (c m) -> p c m", c=8),
            in_=d_wol[l].rearrange("(c p) m -> p c m", p=128)),
            w=[BA(L_WO, 8192)], waits=[prev])
        for t in range(NTL):
            lo = t * TL
            for i in range(4):
                P.dma("sp", "io", lambda e, i=i, lo=lo: e.dma_start(
                    out=ba[:, L_TAB + i * TL: L_TAB + (i + 1) * TL],
                    in_=d_tab[i][:, lo:lo + TL]),
                    w=[BA(L_TAB + i * TL, TL)])
            for c in range(NCH):
                dlo = L_XS + c * TL
                d = ba[:, dlo:dlo + TL]
                sv, rk, bnd, bk = shift_view(c, lo, TL, True)
                if bnd is not None:
                    P.op("dve", lambda e, d=d, s=sv: e.tensor_copy(d[:, 1:TL], s),
                         r=[rk], w=[BA(dlo + 1, TL - 1)])
                    P.op("dve", lambda e, d=d, b_=bnd: e.tensor_copy(d[:, 0:1], b_),
                         r=[bk], w=[BA(dlo, 1)])
                else:
                    P.op("dve", lambda e, d=d, s=sv: e.tensor_copy(d, s),
                         r=[rk], w=[BA(dlo, TL)])
            P.fence([BA(L_XS, 4096)])

            def norm_rot(which, c, par):
                if which == 0:
                    slo = L_Q + c * TL
                    ct, st_ = L_TAB + 0 * TL, L_TAB + 1 * TL
                    scol = 0
                else:
                    slo = L_KR + c * 1024 + (t % 2) * TL
                    ct, st_ = L_TAB + 2 * TL, L_TAB + 3 * TL
                    scol = 1
                src = ba[:, slo:slo + TL]
                sqo = L_SC + par * TL
                sq = ba[:, sqo:sqo + TL]
                stat_bank = 2 + 2 * par
                bc_bank = 3 + 2 * par
                P.op("dve", lambda e, s=src, q=sq: e.tensor_mul(q, s, s),
                     r=[BA(slo, TL)], w=[BA(sqo, TL)])
                mm(stat_bank, psum[stat_bank][0:2, 0:TL], bd2, sq, True, True,
                   r=[BA(sqo, TL), CB(cBD2, 2)])
                ro = (oR0, oR1)[par]
                r2 = fa[0:2, ro:ro + TL]
                P.op("dve", lambda e, r2=r2, sb=stat_bank: e.tensor_scalar_max(
                    r2, psum[sb][0:2, 0:TL], 1e-24), r=[PS(stat_bank)], w=[FA(ro, TL)])
                P.op("act", lambda e, r2=r2: e.activation(r2, r2, AF.Ln),
                     r=[FA(ro, TL)], w=[FA(ro, TL)])
                rbo = cROWS + par * TL
                br = cb[0:2, rbo:rbo + TL]
                P.op("act", lambda e, r2=r2, br=br: e.activation(
                    br, r2, AF.Exp, scale=-0.5), r=[FA(ro, TL)], w=[CB(rbo, TL)])
                bdx = cb[0:2, cBDQ:cBDQ + 128] if which == 0 else \
                    cb[0:2, cBDK:cBDK + 128]
                bdk_ = CB(cBDQ, 128) if which == 0 else CB(cBDK, 128)
                mm(bc_bank, psum[bc_bank][:, 0:TL], bdx, br, True, True,
                   r=[CB(rbo, TL), bdk_])
                ib = ba[:, L_IB:L_IB + TL] if par == 0 else ba[:, L_IB2:L_IB2 + TL]
                ibk = BA(L_IB, TL) if par == 0 else BA(L_IB2, TL)
                P.op("dve", lambda e, ib=ib, bb=bc_bank: e.tensor_copy(
                    ib, psum[bb][:, 0:TL]), r=[PS(bc_bank)], w=[ibk])
                P.op("dve", lambda e, s=src, ib=ib: e.tensor_mul(s, s, ib),
                     r=[ibk, BA(slo, TL)], w=[BA(slo, TL)])
                tro = sqo
                tr = ba[:, tro:tro + TL]
                for (d0, s0) in ((0, 32), (32, 0), (64, 96), (96, 64)):
                    P.op("dve", lambda e, d0=d0, s0=s0, s=src, tr=tr:
                         e.tensor_copy(tr[d0:d0 + 32, :], s[s0:s0 + 32, :]),
                         r=[BA(slo, TL)], w=[BA(tro, TL)])
                P.op("dve", lambda e, tr=tr, st_=st_: e.tensor_mul(
                    tr, tr, ba[:, st_:st_ + TL]),
                    r=[BA(st_, TL), BA(tro, TL)], w=[BA(tro, TL)])
                P.op("dve", lambda e, s=src, ct=ct: e.tensor_mul(
                    s, s, ba[:, ct:ct + TL]),
                    r=[BA(ct, TL), BA(slo, TL)], w=[BA(slo, TL)])
                P.op("dve", lambda e, s=src, tr=tr: e.tensor_add(s, s, tr),
                     r=[BA(tro, TL), BA(slo, TL)], w=[BA(slo, TL)])

            nchain = 0
            for mI in range(24):
                bank = mI % 2
                for k in range(NCH):
                    lhsT = ba[:, L_WQKV + k * 3072 + mI * 128: L_WQKV + k * 3072 + (mI + 1) * 128]
                    rhs = ba[:, L_XS + k * TL: L_XS + (k + 1) * TL]
                    mm(bank, psum[bank][:, 0:TL], lhsT, rhs, k == 0, k == NCH - 1,
                       r=[BA(L_WQKV + k * 3072 + mI * 128, 128), BA(L_XS + k * TL, TL)])
                if mI < 8:
                    dlo = L_Q + mI * TL
                    if mI % 2 == 0:
                        P.op("act", lambda e, dlo=dlo, bank=bank: e.activation(
                            ba[:, dlo:dlo + TL], psum[bank][:, 0:TL], AF.Identity),
                            r=[PS(bank)], w=[BA(dlo, TL)])
                    else:
                        P.op("dve", lambda e, dlo=dlo, bank=bank: e.tensor_copy(
                            ba[:, dlo:dlo + TL], psum[bank][:, 0:TL]),
                            r=[PS(bank)], w=[BA(dlo, TL)])
                    norm_rot(0, mI, nchain % 2)
                    nchain += 1
                elif mI < 16:
                    c = mI - 8
                    dlo = L_KR + c * 1024 + (t % 2) * TL
                    if mI % 2 == 0:
                        P.op("act", lambda e, dlo=dlo, bank=bank: e.activation(
                            ba[:, dlo:dlo + TL], psum[bank][:, 0:TL], AF.Identity),
                            r=[PS(bank)], w=[BA(dlo, TL)])
                    else:
                        P.op("dve", lambda e, dlo=dlo, bank=bank: e.tensor_copy(
                            ba[:, dlo:dlo + TL], psum[bank][:, 0:TL]),
                            r=[PS(bank)], w=[BA(dlo, TL)])
                    norm_rot(1, c, nchain % 2)
                    nchain += 1
                else:
                    c = mI - 16
                    P.op("act", lambda e, bank=bank: e.activation(
                        ba[:, L_VS:L_VS + TL], psum[bank][:, 0:TL], AF.Identity),
                        r=[PS(bank)], w=[BA(L_VS, TL)])
                    P.last_ldw = None
                    for cb4 in range(4):
                        kb = (t * 4 + cb4) % 8
                        pslot = (cb4 % 2) * 128
                        P.op("pe", lambda e, cb4=cb4, pslot=pslot:
                             e.transpose(psbf[:, pslot:pslot + 128],
                                         ba[:, L_VS + cb4 * 128:L_VS + (cb4 + 1) * 128],
                                         ident),
                             r=[BA(L_VS, TL), CB(cID, 128)], w=[PS(7)])
                        for h2 in range(2):
                            off = L_VN + (2 * c + h2) * 520 + kb * 65
                            P.op("dve", lambda e, off=off, pslot=pslot, h2=h2:
                                 e.tensor_copy(ba[:, off:off + 64],
                                               psbf[:, pslot + h2 * 64:pslot + h2 * 64 + 64]),
                                 r=[PS(7)], w=[BA(off, 64)])
            P.fence([BA(L_Q, 4096), BA(L_KR, 8192), BA(L_VN, 8320)])
            for wi in range(2):
                wloc = t * 2 + wi
                qcol = wi * 256
                kbs = [x for x in (2 * wloc - 2, 2 * wloc - 1, 2 * wloc, 2 * wloc + 1)
                       if x >= 0]
                mks = list(range(4 - len(kbs), 4))
                for hp in range(8):
                    def st_mm(j):
                        kb = kbs[j]
                        slot = kb % 8
                        bA, bB = j % 2, 2 + j % 2
                        klo = L_KR + hp * 1024 + slot * 128
                        qap_ = L_Q + hp * TL + qcol
                        mm(bA, psum[bA][:, 0:256], ba[0:64, klo:klo + 128],
                           ba[0:64, qap_:qap_ + 256], True, False,
                           r=[BA(klo, 128), BA(L_Q + hp * TL, TL)])
                        mm(bB, psum[bB][:, 0:256], ba[64:128, klo:klo + 128],
                           ba[64:128, qap_:qap_ + 256], True, False, tp=(64, 0),
                           r=[BA(klo, 128), BA(L_Q + hp * TL, TL)])
                        mk, mkr = mask_ap("local", mks[j])
                        mm(bA, psum[bA][:, 0:256], ident, mk, False, True,
                           r=[CB(cID, 128), mkr], ldw_key="ident")
                        mm(bB, psum[bB][:, 0:256], ident, mk, False, True,
                           r=[CB(cID, 128), mkr], ldw_key="ident")
                        pA = L_PT + (j % 2) * 256
                        pB = L_PT + 512 + (j % 2) * 256
                        P.op("act", lambda e, pA=pA, bA=bA: e.activation(
                            ba[:, pA:pA + 256], psum[bA][:, 0:256], AF.Exp, scale=8.0),
                            r=[PS(bA)], w=[BA(pA, 256)])
                        P.op("act", lambda e, pB=pB, bB=bB: e.activation(
                            ba[:, pB:pB + 256], psum[bB][:, 0:256], AF.Exp, scale=8.0),
                            r=[PS(bB)], w=[BA(pB, 256)])

                    def pv_mm(j):
                        kb = kbs[j]
                        slot = kb % 8
                        pA = L_PT + (j % 2) * 256
                        pB = L_PT + 512 + (j % 2) * 256
                        for h2, (pof, ob) in enumerate(((pA, 4), (pB, 5))):
                            vno = L_VN + (2 * hp + h2) * 520 + slot * 65
                            mm(ob, psum[ob][0:65, 0:256], ba[:, vno:vno + 65],
                               ba[:, pof:pof + 256], j == 0, j == len(kbs) - 1,
                               r=[BA(vno, 65), BA(pof, 256)])

                    st_mm(0)
                    for j in range(len(kbs)):
                        if j + 1 < len(kbs):
                            st_mm(j + 1)
                        pv_mm(j)
                    par = hp % 2
                    oo = L_OW + hp * 256
                    for h2, ob in enumerate((4, 5)):
                        P.op("dve", lambda e, oo=oo, h2=h2, ob=ob: e.tensor_copy(
                            ba[h2 * 64:(h2 + 1) * 64, oo:oo + 256],
                            psum[ob][0:64, 0:256]),
                            r=[PS(ob)], w=[BA(oo, 256)])
                        ro = (oR0, oR1)[par]
                        rr = fa[0:1, ro:ro + 256]
                        P.op("act", lambda e, ob=ob, rr=rr: e.activation(
                            rr, psum[ob][64:65, 0:256], AF.Ln), r=[PS(ob)], w=[FA(ro, 256)])
                        rbo = cROWS + (par * 2 + h2) * TL
                        rb = cb[0:1, rbo:rbo + 256]
                        P.op("act", lambda e, rr=rr, rb=rb: e.activation(
                            rb, rr, AF.Exp, scale=-1.0), r=[FA(ro, 256)], w=[CB(rbo, 256)])
                        mm(6, psum[6][0:64, 0:256], ones_row[0:1, 0:64], rb, True, True,
                           r=[CB(rbo, 256), CB(cONES, 128)])
                        ibo = (L_IB, L_IB2)[h2]
                        ib = ba[h2 * 64:(h2 + 1) * 64, ibo:ibo + 256]
                        P.op("dve", lambda e, ib=ib: e.tensor_copy(
                            ib, psum[6][0:64, 0:256]), r=[PS(6)], w=[BA(ibo, 256)])
                        P.op("dve", lambda e, oo=oo, h2=h2, ib=ib: e.tensor_mul(
                            ba[h2 * 64:(h2 + 1) * 64, oo:oo + 256],
                            ba[h2 * 64:(h2 + 1) * 64, oo:oo + 256], ib),
                            r=[BA(ibo, 256), BA(oo, 256)], w=[BA(oo, 256)])
                P.fence([BA(L_OW, 2048)])
                for m in range(NCH):
                    bank = m % 2
                    for k in range(NCH):
                        lhsT = ba[:, L_WO + k * 1024 + m * 128: L_WO + k * 1024 + (m + 1) * 128]
                        rhs = ba[:, L_OW + k * 256: L_OW + (k + 1) * 256]
                        mm(bank, psum[bank][:, 0:256], lhsT, rhs, k == 0, k == NCH - 1,
                           r=[BA(L_WO + k * 1024 + m * 128, 128), BA(L_OW + k * 256, 256)])
                    col = lo + qcol
                    tgt = FX(m)[:, col:col + 256]
                    P.op("dve", lambda e, tgt=tgt, bank=bank: e.tensor_add(
                        tgt, psum[bank][:, 0:256], tgt),
                        r=[PS(bank), FA(oXR + m * S + col, 256)],
                        w=[FA(oXR + m * S + col, 256)])
    def emit_ffn(l, n_local):
        emit_stash_bnd()
        for ks in range(NKS):
            w = 128 if ks < 21 else 42
            P.dma("pool", "w", lambda e, l=l, ks=ks, w=w: e.dma_start(
                out=ba[0:w, F_W2 + ks * 1024: F_W2 + ks * 1024 + 1024],
                in_=d_w2[l][ks * 128: ks * 128 + w, :]),
                w=[BA(F_W2 + ks * 1024, 1024)])
        for tp in range(2):
            tiles = [2 * tp, 2 * tp + 1]
            emit_ln(tiles, lambda ti, c: F_XN + c * 1024 + ti * TL, shifted=True,
                    layer=l, sq_off=F_SQ, mu_off=F_MU, isd_off=F_ISD)
            for j in range(NKS):
                w = 128 if j < 21 else 42
                p_ = j % 2
                bkA = (0, 1, 2, 3) if p_ == 0 else (4, 5, 6, 2)
                for half in range(2):
                    cofs = j * 128 + half * DI
                    ring = F_W1 + ((2 * j + half) % 4) * 1024
                    P.dma("pool", "w", lambda e, l=l, cofs=cofs, w=w, ring=ring:
                          e.dma_start(
                              out=ba[:, ring:ring + 8 * w].rearrange(
                                  "p (c m) -> p c m", c=8),
                              in_=d_w1[l][:, cofs:cofs + w].rearrange(
                                  "(c p) m -> p c m", p=128)),
                          w=[BA(ring, 1024)])
                    for k in range(NCH):
                        lhsT = ba[:, ring + k * w: ring + (k + 1) * w]
                        for t2 in range(2):
                            bank = bkA[half * 2 + t2]
                            rhs = ba[:, F_XN + k * 1024 + t2 * TL: F_XN + k * 1024 + (t2 + 1) * TL]
                            mm(bank, psum[bank][0:w, 0:TL], lhsT, rhs,
                               k == 0, k == NCH - 1,
                               r=[BA(ring + k * w, w), BA(F_XN + k * 1024 + t2 * TL, TL)],
                               ldw_key=("w1", j, half, k))
                for t2 in range(2):
                    w_ = w
                    hoff = F_H + t2 * 11264 + j * TL
                    gt = ba[0:w_, F_GT + p_ * TL:F_GT + p_ * TL + TL]
                    gb = bkA[2 + t2]
                    ab = bkA[t2]
                    P.op("act", lambda e, gt=gt, gb=gb, w_=w_, j=j: e.activation(
                        gt, psum[gb][0:w_, 0:TL], AF.Gelu,
                        bias=smcol(48 + j, w_)),
                        r=[PS(gb), SM_R], w=[BA(F_GT + p_ * TL, TL)])
                    P.op("act", lambda e, hoff=hoff, ab=ab, w_=w_, j=j: e.activation(
                        ba[0:w_, hoff:hoff + TL], psum[ab][0:w_, 0:TL], AF.Identity,
                        bias=smcol(26 + j, w_)),
                        r=[PS(ab), SM_R], w=[BA(hoff, TL)])
                    P.op("dve", lambda e, hoff=hoff, gt=gt, w_=w_: e.tensor_mul(
                        ba[0:w_, hoff:hoff + TL], ba[0:w_, hoff:hoff + TL], gt),
                        r=[BA(F_GT + p_ * TL, TL), BA(hoff, TL)], w=[BA(hoff, TL)])
            P.fence([BA(F_H, 22528)])
            for m in range(NCH):
                bks = ((2, 3), (4, 5))[m % 2]
                for ks in range(NKS):
                    w = 128 if ks < 21 else 42
                    lhsT = ba[0:w, F_W2 + ks * 1024 + m * 128: F_W2 + ks * 1024 + (m + 1) * 128]
                    for t2 in range(2):
                        rhs = ba[0:w, F_H + t2 * 11264 + ks * TL: F_H + t2 * 11264 + (ks + 1) * TL]
                        mm(bks[t2], psum[bks[t2]][:, 0:TL], lhsT, rhs,
                           ks == 0, ks == NKS - 1,
                           r=[BA(F_W2 + ks * 1024 + m * 128, 128),
                              BA(F_H + t2 * 11264 + ks * TL, TL)],
                           ldw_key=("w2", m, ks))
                for t2 in range(2):
                    col = (2 * tp + t2) * TL
                    tgt = FX(m)[:, col:col + TL]
                    P.op("dve", lambda e, tgt=tgt, bank=bks[t2], m=m: e.scalar_tensor_tensor(
                        tgt, psum[bank][:, 0:TL], smcol(18 + m), tgt, ALU.add, ALU.add),
                        r=[PS(bks[t2]), SM_R, FA(oXR + m * S + col, TL)],
                        w=[FA(oXR + m * S + col, TL)])
    def emit_stash():
        for c in range(NCH):
            P.dma("sp", "io", lambda e, c=c: e.dma_start(out=d_x1s[c], in_=FX(c)),
                  r=[FA(oXR + c * S, S)])

    def emit_final_add():
        for c in range(NCH):
            for piece in range(4):
                lo = piece * TL
                ro = (oR0, oR1)[(c * 4 + piece) % 2]
                st = fa[:, ro:ro + TL]
                P.dma("sp", "io", lambda e, c=c, lo=lo, st=st: e.dma_start(
                    out=st, in_=d_x1s[c][:, lo:lo + TL]), w=[FA(ro, TL)])
                P.op("dve", lambda e, c=c, lo=lo, st=st: e.tensor_add(
                    FX(c)[:, lo:lo + TL], FX(c)[:, lo:lo + TL], st),
                    r=[FA(ro, TL), FA(oXR + c * S + lo, TL)],
                    w=[FA(oXR + c * S + lo, TL)])

    tapn = [0]

    def emit_tap():
        if not n_taps or tapn[0] >= n_taps:
            return
        i = tapn[0]
        tapn[0] += 1
        for c in range(NCH):
            P.dma("sp", "io", lambda e, i=i, c=c: e.dma_start(
                out=d_dbg[i, c], in_=FX(c)), r=[FA(oXR + c * S, S)])

    final_toks = []
    for b in range(NB):
        emit_load_x(b)
        if cfg["with_main"]:
            emit_ln([0, 1, 2, 3], lambda ti, c: M_XN + c * S + ti * TL,
                    shifted=False, sq_off=M_SQ, mu_off=M_MU, isd_off=M_ISD)
            emit_main_attn()
        if cfg["with_slot"]:
            emit_stash()
            emit_ln([0, 1, 2, 3], lambda ti, c: E_XC + c * TL, shifted=False,
                    inplace=True, sq_off=E_SQ, mu_off=E_MU, isd_off=E_ISD)
            if cfg["n_local"] > 0:
                load_masks("local")
            for l in range(cfg["n_local"]):
                load_smalls(l)
                emit_local_attn(l)
                emit_tap()
                emit_ffn(l, cfg["n_local"])
                emit_tap()
            emit_final_add()
        final_toks += emit_store(b)

    P.op("sp", lambda e: None, waits=final_toks)

    P.finalize(nc, stack)
    nops = {e: len(P.ops[e]) for e in ENGS}
    nwaits = {e: sum(len(o.waits) for o in P.ops[e]) for e in ENGS}
    print("[kernel] op counts:", nops, "total", sum(nops.values()))
    print("[kernel] raw wait edges:", nwaits)

    with nc.Block() as blk:
        @blk.tensor
        def _(e):
            P.replay("pe", e)

        @blk.scalar
        def _(e):
            P.replay("act", e)

        @blk.vector
        def _(e):
            P.replay("dve", e)

        @blk.gpsimd
        def _(e):
            P.replay("pool", e)

        @blk.sync
        def _(e):
            P.replay("sp", e)

    stack.close()
    return nc


_PROG_CACHE = {}


def _get_program():
    key = tuple(sorted(_CFG.items()))
    if key not in _PROG_CACHE:
        _PROG_CACHE[key] = build_program(dict(_CFG))
    return _PROG_CACHE[key]


def kernel(L0_attn_x, L0_attn_WQ, L0_attn_WK, L0_attn_WV, L0_attn_WO,
           L0_ffn_Wqkv, L0_ffn_q_scale, L0_ffn_k_scale, L0_ffn_Wo,
           L0_ffn_ln_g, L0_ffn_ln_b, L0_ffn_W1, L0_ffn_b1, L0_ffn_W2,
           L0_ffn_b2):
    x = np.ascontiguousarray(np.asarray(L0_attn_x, np.float32))
    WQ = np.asarray(L0_attn_WQ, np.float32)
    WK = np.asarray(L0_attn_WK, np.float32)
    WV = np.asarray(L0_attn_WV, np.float32)
    WO = np.asarray(L0_attn_WO, np.float32)

    wqt = np.ascontiguousarray(WQ.reshape(8, 128, D).transpose(0, 2, 1))
    wkt = np.ascontiguousarray(WK.reshape(8, 128, D).transpose(0, 2, 1))
    wvt = np.ascontiguousarray(WV.reshape(8, 128, D).transpose(0, 2, 1))
    wot = np.ascontiguousarray(WO.transpose(0, 2, 1).reshape(D, D))

    qs = np.asarray(L0_ffn_q_scale, np.float32)
    ks = np.asarray(L0_ffn_k_scale, np.float32)
    b1 = np.asarray(L0_ffn_b1, np.float32)
    b2 = np.asarray(L0_ffn_b2, np.float32)
    lng = np.asarray(L0_ffn_ln_g, np.float32)
    lnb = np.asarray(L0_ffn_ln_b, np.float32)

    sm = np.zeros((6, 128, 72), np.float32)
    qs2 = np.concatenate([qs, qs], axis=1)
    ks2 = np.concatenate([ks, ks], axis=1)
    sm[:, :, 0] = qs2
    sm[:, :, 1] = ks2
    qkrow = np.zeros((6, 2, 2, 128), np.float32)
    qkrow[:, 0, 0] = qs2
    qkrow[:, 0, 1] = qs2
    qkrow[:, 1, 0] = ks2
    qkrow[:, 1, 1] = ks2
    sm[:, :, 2:10] = _cols(lng, 8)
    sm[:, :, 10:18] = _cols(lnb, 8)
    sm[:, :, 18:26] = _cols(b2, 8)
    sm[:, :, 26:48] = _cols(b1[:, :DI], NKS)
    sm[:, :, 48:70] = _cols(b1[:, DI:], NKS)

    shared = {
        "wqt": wqt, "wkt": wkt, "wvt": wvt, "wot": wot,
        "wqkv": np.ascontiguousarray(np.asarray(L0_ffn_Wqkv, np.float32)),
        "wol": np.ascontiguousarray(np.asarray(L0_ffn_Wo, np.float32)),
        "w1": np.ascontiguousarray(np.asarray(L0_ffn_W1, np.float32)),
        "w2": np.ascontiguousarray(np.asarray(L0_ffn_W2, np.float32)),
        "sm": sm,
        "tab": _rotary_tables(),
        "mloc": _local_masks(),
        "mmain": _main_masks(),
        "ident": _bf(np.eye(128, dtype=np.float32)),
        "bd2": _bf(np.kron(np.eye(2), np.ones((64, 1)))),
        "bd2t": _bf(np.kron(np.eye(2), np.ones((1, 64)))),
        "onesr": _bf(np.ones((1, 512), np.float32)),
        "qkrow": qkrow,
    }

    in_maps = []
    for c in range(NCORES):
        shard = x[2 * c:2 * c + 2]
        xs = np.ascontiguousarray(
            shard.transpose(0, 2, 1).reshape(NB, NCH, 128, S))
        in_maps.append({"x": xs, **shared})

    nc = _get_program()
    res = run_bass_kernel_spmd(nc, in_maps, core_ids=list(range(NCORES)))
    outs = []
    for c in range(NCORES):
        o = res.results[c]["out"]
        outs.append(np.asarray(o, np.float32).reshape(NB, D, S).transpose(0, 2, 1))
    if _CFG.get("taps"):
        global _TAPS
        _TAPS = [np.asarray(res.results[c]["dbg"], np.float32) for c in range(NCORES)]
    return np.ascontiguousarray(np.concatenate(outs, axis=0))
